# version 16
# baseline (speedup 1.0000x reference)
import numpy as np
import ml_dtypes
import concourse.bass as bass
import concourse.mybir as mybir
import concourse.bass_utils as _bu
from concourse.bass_utils import run_bass_kernel_spmd

if not getattr(_bu, "_ldw_patched", False):
    _orig_run_command = _bu.run_command

    def _run_command(argv, **kwargs):
        argv = ["--enable-ldw-opt=true" if a == "--enable-ldw-opt=false" else a
                for a in argv]
        return _orig_run_command(argv, **kwargs)

    _bu.run_command = _run_command
    _bu._ldw_patched = True

F32 = mybir.dt.float32
BF16 = mybir.dt.bfloat16
I32 = mybir.dt.int32
AX = mybir.AxisListType.X
AF = mybir.ActivationFunctionType
ALU = mybir.AluOpType
BF16NP = ml_dtypes.bfloat16

MAGIC = float(1.5 * 2 ** 23)
EXPC = 16.0
RSQ = float(1.0 / np.sqrt(128.0))
EPS1 = 1e-5
EPS2 = 1e-8


class Cfg:
    def __init__(self, NT=32, HT=64, GT=4):
        self.NT = NT
        self.NO = NT // 2
        self.EC = 16
        self.H = 16
        self.G = 4
        self.HT = HT
        self.DT = 16
        self.GT = GT
        self.NG = self.NO // GT
        self.E = self.EC * 128
        self.FH = HT * 128


class Op:
    __slots__ = ("eng", "emit", "deps", "need_inc", "sem_val", "dma_q", "idx")

    def __init__(self, eng, emit, dma_q, idx):
        self.eng = eng
        self.emit = emit
        self.deps = []
        self.need_inc = dma_q is not None
        self.sem_val = None
        self.dma_q = dma_q
        self.idx = idx


class Prog:

    ENGS = ("Y", "S", "V", "T", "G")
    NPOOL = 8

    def __init__(self, nc):
        self.nc = nc
        self.ops = []
        self.writer = {}
        self.readers = {}
        self.last_on = {}
        self.barrier_deps = None
        self.barrier_seen = set()
        self.qidx = {"dY": 0, "dA": 0, "dG": 0}
        self.counts = {}
        self.throttle = {}

    def barrier(self):
        self.barrier_deps = [o for o in self.last_on.values()]
        self.barrier_seen = set()

    def op(self, eng, emit, reads=(), writes=(), waw_free=False, dma_q=None):
        rec = Op(eng, emit, dma_q, len(self.ops))
        lane = eng
        if dma_q is not None:
            k = self.qidx[dma_q]
            self.qidx[dma_q] = k + 1
            slot = (dma_q, k % self.NPOOL)
            v = self.counts.get(slot, 0) + 16
            self.counts[slot] = v
            rec.sem_val = (slot, v)
            if v > 16:
                self.throttle[rec.idx] = (slot, v - 16)
            lane = slot
        deps = {}

        def dep(pr):
            if pr is not None and pr is not rec:
                deps[pr.idx] = pr

        if self.barrier_deps is not None:
            if lane not in self.barrier_seen:
                self.barrier_seen.add(lane)
                for pr in self.barrier_deps:
                    dep(pr)
        for k in reads:
            dep(self.writer.get(k))
        for k in writes:
            w = self.writer.get(k)
            if w is not None and not (waw_free and w.eng == "T" and eng == "T"
                                      and w.dma_q is None and dma_q is None):
                dep(w)
            for r in self.readers.get(k, ()):
                dep(r)
        for k in reads:
            self.readers.setdefault(k, []).append(rec)
        for k in writes:
            self.writer[k] = rec
            self.readers[k] = []
        for pr in deps.values():
            pr.need_inc = True
            rec.deps.append(pr)
        self.ops.append(rec)
        self.last_on[lane] = rec
        return rec

    def finalize(self):
        nc = self.nc
        sems = {}
        for e in self.ENGS:
            sems[e] = nc.alloc_semaphore(f"sem_{e}")
        for q in ("dY", "dA", "dG"):
            for i in range(self.NPOOL):
                sems[(q, i)] = nc.alloc_semaphore(f"sem_{q}{i}")
        counts = dict.fromkeys(sems, 0)
        counts.update(self.counts)
        throttle = self.throttle
        for rec in self.ops:
            if rec.dma_q is None and rec.need_inc:
                counts[rec.eng] += 1
                rec.sem_val = (rec.eng, counts[rec.eng])
        totals = dict(counts)
        by_eng = {e: [] for e in self.ENGS}
        for rec in self.ops:
            by_eng[rec.eng].append(rec)

        with nc.Block() as block:
            def emit_engine(ename, engobj):
                water = dict.fromkeys(sems, 0)
                for rec in by_eng[ename]:
                    for pr in rec.deps:
                        s, v = pr.sem_val
                        if water[s] < v:
                            engobj.wait_ge(sems[s], v)
                            water[s] = v
                    th = throttle.get(rec.idx)
                    if th is not None and water[th[0]] < th[1]:
                        engobj.wait_ge(sems[th[0]], th[1])
                        water[th[0]] = th[1]
                    insts = rec.emit(engobj)
                    if rec.sem_val is not None:
                        s, v = rec.sem_val
                        last = insts[-1] if isinstance(insts, (list, tuple)) else insts
                        last.then_inc(sems[s], 16 if rec.dma_q else 1)
                for q, owner in (("dY", "Y"), ("dA", "S"), ("dG", "G")):
                    if owner == ename:
                        for i in range(self.NPOOL):
                            if totals[(q, i)] > 0 and water[(q, i)] < totals[(q, i)]:
                                engobj.wait_ge(sems[(q, i)], totals[(q, i)])

            for ename, secname in (("Y", "sync"), ("S", "scalar"),
                                   ("V", "vector"), ("T", "tensor"),
                                   ("G", "gpsimd")):
                getattr(block, secname)(
                    lambda eo, _n=ename: emit_engine(_n, eo))
        return totals


def build_program(cfg: Cfg):
    nc = bass.Bass("TRN2", target_bir_lowering=False)
    NT, NO, EC, H, G, HT, DT, GT, NG = (cfg.NT, cfg.NO, cfg.EC, cfg.H,
                                        cfg.G, cfg.HT, cfg.DT, cfg.GT, cfg.NG)
    E, FH = cfg.E, cfg.FH
    NTOK = GT * 128

    d_x = nc.dram_tensor("x_full", [NT * 128, E], F32, kind="ExternalInput").ap()
    d_xo = nc.dram_tensor("x_own", [NO, 128, E], F32, kind="ExternalInput").ap()
    d_wq = nc.dram_tensor("wq_in", [EC, 128, H * 128], F32, kind="ExternalInput").ap()
    d_wkv = nc.dram_tensor("wkv_in", [EC, 128, 2 * G * 128], F32, kind="ExternalInput").ap()
    d_bq = nc.dram_tensor("bqT", [128, H], F32, kind="ExternalInput").ap()
    d_bk = nc.dram_tensor("bkT", [128, G], F32, kind="ExternalInput").ap()
    d_bv = nc.dram_tensor("bv_bcast", [128, G * 128], F32, kind="ExternalInput").ap()
    d_wg = nc.dram_tensor("w_g", [FH, E], F32, kind="ExternalInput").ap()
    d_wu = nc.dram_tensor("w_u", [FH, E], F32, kind="ExternalInput").ap()
    d_wd = nc.dram_tensor("w_d", [E, FH], F32, kind="ExternalInput").ap()
    d_mask = nc.dram_tensor("mask_tab", [NO, 128, 1024], BF16, kind="ExternalInput").ap()
    d_identb = nc.dram_tensor("identb", [128, 128], BF16, kind="ExternalInput").ap()
    d_identf = nc.dram_tensor("identf", [128, 128], F32, kind="ExternalInput").ap()
    d_onesb = nc.dram_tensor("onesb", [128, 8], BF16, kind="ExternalInput").ap()
    d_c127 = nc.dram_tensor("c127", [128, 8], F32, kind="ExternalInput").ap()

    d_vs = nc.dram_tensor("v_scr", [NT, 128, G * 128], BF16).ap()
    d_x1 = nc.dram_tensor("x1_scr", [NO, 128, E], F32).ap()
    d_sg = nc.dram_tensor("swg_scr", [HT, 128, E], BF16).ap()
    d_su = nc.dram_tensor("swu_scr", [HT, 128, E], BF16).ap()
    d_sd = nc.dram_tensor("swd_scr", [DT, 128, FH], BF16).ap()
    d_out = nc.dram_tensor("out", [NG, DT, 128, NTOK], F32, kind="ExternalOutput").ap()

    p = Prog(nc)
    SB = nc.alloc_sbuf_tensor
    PS = nc.alloc_psum_tensor

    identb = SB("sb_identb", [128, 128], BF16)
    onesb = SB("sb_onesb", [128, 8], BF16)
    c127 = SB("sb_c127", [128, 8], F32)
    bqs = SB("sb_bqs", [128, H], F32)
    bkT = SB("sb_bkT", [128, G], F32)
    xst = [SB(f"xst{i}", [128, E], F32) for i in range(2)]
    stats = SB("stats", [128, 32], F32)
    st2 = SB("st2", [128, 12], F32)

    KEY_B = [("PB", i) for i in range(8)]
    psum0 = (nc.psum_base, nc.psum_top)

    def psum_reset():
        nc.psum_base, nc.psum_top = psum0

    kvA = PS("kvA", [128, 1024], F32)
    kvB = PS("kvB", [128, 1024], F32)
    xtr1 = PS("xtr1", [128, 2048], BF16)
    ktr1 = PS("ktr1", [128, 1024], BF16)
    wtr = PS("wtr", [128, 1024], BF16)
    K_kvA, K_kvB = KEY_B[0:2], KEY_B[2:4]
    K_xtr1, K_ktr1, K_wtr = KEY_B[4:6], [KEY_B[6]], [KEY_B[7]]

    sb_savedA = (nc.sbuf_base, nc.sbuf_top)
    xln_tm = [SB(f"xln_tm{i}", [128, E], BF16) for i in range(2)]
    xln_fm = [SB(f"xln_fm{i}", [128, E], BF16) for i in range(2)]
    Kd = [SB(f"Kd{g}", [128, NT * 128], BF16) for g in range(G)]
    wst = [SB(f"wst{i}", [128, E], BF16) for i in range(2)]
    wtg = SB("wtg0", [128, E], BF16)
    sb_savedB = (nc.sbuf_base, nc.sbuf_top)
    Wkv_sb = SB("Wkv_sb", [128, EC, 2 * G * 128], BF16)
    ktm = SB("ktm", [128, G * 128], BF16)
    vtl = [SB(f"vtl{i}", [128, G * 128], BF16) for i in range(2)]
    bvb = SB("sb_bvb", [128, G * 128], F32)

    def ln_chain(src_key, src_ap, eps_col, out_ap, out_key):
        nchunk = E // 512
        p.op("V", lambda v: [v.bn_stats(stats[:, 6 * c:6 * c + 6],
                                        src_ap[:, 512 * c:512 * c + 512])
                             for c in range(nchunk)],
             reads=[src_key], writes=[("bn",)])
        p.op("V", lambda v: v.bn_aggr(st2[:, 0:2], stats[:, 0:6 * nchunk]),
             reads=[("bn",)], writes=[("mv",)])
        p.op("S", lambda s: s.activation(st2[:, 2:3], st2[:, 1:2], AF.Sqrt,
                                         bias=c127[:, eps_col:eps_col + 1]),
             reads=[("mv",), ("c127",)], writes=[("std",)])
        p.op("V", lambda v: v.reciprocal(st2[:, 3:4], st2[:, 2:3]),
             reads=[("std",)], writes=[("rstd",)])
        p.op("V", lambda v: v.tensor_scalar(out_ap, src_ap, st2[:, 0:1],
                                            st2[:, 3:4], ALU.subtract,
                                            ALU.mult),
             reads=[src_key, ("mv",), ("rstd",)], writes=[out_key])

    def ln_chain_s(src_key, src_ap, eps_col, out_ap, out_key):
        nchunk = E // 512
        p.op("V", lambda v: [v.bn_stats(stats[:, 6 * c:6 * c + 6],
                                        src_ap[:, 512 * c:512 * c + 512])
                             for c in range(nchunk)],
             reads=[src_key], writes=[("bn",)])
        p.op("V", lambda v: v.bn_aggr(st2[:, 0:2], stats[:, 0:6 * nchunk]),
             reads=[("bn",)], writes=[("mv",)])
        p.op("S", lambda s: s.activation(st2[:, 2:3], st2[:, 1:2], AF.Sqrt,
                                         bias=c127[:, eps_col:eps_col + 1]),
             reads=[("mv",), ("c127",)], writes=[("std",)])
        p.op("V", lambda v: v.reciprocal(st2[:, 3:4], st2[:, 2:3]),
             reads=[("std",)], writes=[("rstd",)])
        p.op("V", lambda v: v.tensor_scalar(st2[:, 8:9], st2[:, 0:1],
                                            st2[:, 3:4], -1.0,
                                            ALU.mult, ALU.mult),
             reads=[("mv",), ("rstd",)], writes=[("negmr",)])
        p.op("S", lambda s: s.activation(out_ap, src_ap, AF.Identity,
                                         bias=st2[:, 8:9], scale=st2[:, 3:4]),
             reads=[src_key, ("negmr",), ("rstd",)], writes=[out_key])

    def transpose16(src_ap, src_key, ps, ps_keys, dst_ap, dst_key):
        p.op("T", lambda t: [t.transpose(ps[:, 128 * c:128 * c + 128],
                                         src_ap[:, 128 * c:128 * c + 128],
                                         identb[:]) for c in range(EC)],
             reads=[src_key, ("identb",)], writes=ps_keys)
        p.op("V", lambda v: v.tensor_copy(dst_ap, ps[:, 0:128 * EC]),
             reads=ps_keys, writes=[dst_key])

    for sb_t, key, src in ((identb, "identb", d_identb),
                           (onesb, "onesb", d_onesb), (c127, "c127", d_c127),
                           (bqs, "bqs", d_bq), (bkT, "bkT", d_bk),
                           (bvb, "bvb", d_bv)):
        p.op("Y", lambda y, sb_t=sb_t, src=src: y.dma_start(out=sb_t[:], in_=src),
             writes=[(key,)], dma_q="dY")
    p.op("S", lambda s: s.mul(bqs[:], bqs[:], RSQ),
         reads=[("bqs",)], writes=[("bqs",)])
    for ech in range(EC):
        p.op("G", lambda g, c=ech: g.dma_start(out=Wkv_sb[:, c, :], in_=d_wkv[c]),
             writes=[("wkv", ech)], dma_q="dG")

    prep_srcs = []

    def prep_issue(k):
        src_ap, scr_ap, ncol, scr_key = prep_srcs[k]
        wb = k % 2
        p.op("G", lambda g, wb=wb, ncol=ncol, src_ap=src_ap: g.dma_start(
                out=wst[wb][:, 0:ncol], in_=src_ap),
             writes=[("wst", wb)], dma_q="dG")

    def prep_compute(k):
        src_ap, scr_ap, ncol, scr_key = prep_srcs[k]
        wb = k % 2
        nb = ncol // 128
        for h0 in range(0, nb, 8):
            hn = min(8, nb - h0)
            p.op("T", lambda t, h0=h0, hn=hn, wb=wb: [t.transpose(
                    wtr[:, 128 * c:128 * c + 128],
                    wst[wb][:, 128 * (h0 + c):128 * (h0 + c) + 128],
                    identb[:]) for c in range(hn)],
                 reads=[("wst", wb), ("identb",)], writes=K_wtr)
            p.op("S", lambda s, h0=h0, hn=hn: s.activation(
                    wtg[:, 128 * h0:128 * (h0 + hn)], wtr[:, 0:128 * hn],
                    AF.Sign),
                 reads=K_wtr, writes=[("wtg",)])
        p.op("G", lambda g, ncol=ncol, scr_ap=scr_ap: g.dma_start(
                out=scr_ap, in_=wtg[:, 0:ncol]),
             reads=[("wtg",)], writes=[scr_key], dma_q="dG")

    for ht in range(HT):
        prep_srcs.append((d_wg[128 * ht:128 * ht + 128, :], d_sg[ht], E,
                          ("dsg", ht)))
        prep_srcs.append((d_wu[128 * ht:128 * ht + 128, :], d_su[ht], E,
                          ("dsu", ht)))
    ncol_d = min(E, FH)
    for dt in range(DT):
        for hcg in range(max(1, FH // ncol_d)):
            prep_srcs.append((
                d_wd[128 * dt:128 * dt + 128, ncol_d * hcg:ncol_d * (hcg + 1)],
                d_sd[dt][:, ncol_d * hcg:ncol_d * (hcg + 1)], ncol_d,
                ("dsd", dt, hcg)))
    n_prep = len(prep_srcs)
    state = {"issued": 0, "done": 0}

    def drain_prep(k):
        while k > 0 and state["done"] < n_prep:
            while state["issued"] < min(n_prep, state["done"] + 2):
                prep_issue(state["issued"])
                state["issued"] += 1
            prep_compute(state["done"])
            state["done"] += 1
            k -= 1

    per = -(-n_prep // (NT + NO - 2)) if (NT + NO) > 2 else n_prep

    x_t = d_x.rearrange("(n p) e -> n p e", p=128)

    def ph1_head(t):
        b = t % 2
        p.op("Y", lambda y, t=t, b=b: y.dma_start(out=xst[b][:], in_=x_t[t]),
             writes=[("xst", b)], dma_q="dY")
        ln_chain(("xst", b), xst[b][:], 1, xln_tm[b][:], ("xln_tm", b))

    ph1_head(0)
    for t in range(NT):
        b = t % 2
        transpose16(xln_tm[b][:], ("xln_tm", b), xtr1, K_xtr1, xln_fm[b][:],
                    ("xln_fm", b))
        if t + 1 < NT:
            ph1_head(t + 1)
        kv = kvA if b == 0 else kvB
        K_kv = K_kvA if b == 0 else K_kvB

        def kv_mms(te, kv=kv, b=b):
            out = []
            for c in range(EC):
                for half in range(2):
                    out.append(te.matmul(
                        kv[:, 512 * half:512 * half + 512],
                        xln_fm[b][:, 128 * c:128 * c + 128],
                        Wkv_sb[:, c, 512 * half:512 * half + 512],
                        start=(c == 0), stop=(c == EC - 1)))
            return out
        p.op("T", kv_mms, reads=[("xln_fm", b)] + [("wkv", c) for c in range(EC)],
             writes=K_kv)
        p.op("S", lambda s, kv=kv: s.activation(ktm[:], kv[:, 0:512], AF.Copy),
             reads=K_kv, writes=[("ktm",)])
        p.op("T", lambda te: [te.transpose(ktr1[:, 128 * g:128 * g + 128],
                                           ktm[:, 128 * g:128 * g + 128],
                                           identb[:]) for g in range(G)],
             reads=[("ktm",), ("identb",)], writes=K_ktr1)
        p.op("S", lambda s, t=t: [s.activation(
                Kd[g][:, 128 * t:128 * t + 128],
                ktr1[:, 128 * g:128 * g + 128],
                AF.Identity, bias=bkT[:, g:g + 1]) for g in range(G)],
             reads=K_ktr1 + [("bkT",)], writes=[("Kd", t)])
        p.op("V", lambda v, kv=kv, b=b: v.tensor_tensor(vtl[b][:], kv[:, 512:1024],
                                                   bvb[:], ALU.add),
             reads=K_kv + [("bvb",)], writes=[("vtl", b)])
        p.op("S", lambda g, t=t, b=b: g.dma_start(out=d_vs[t], in_=vtl[b][:]),
             reads=[("vtl", b)], writes=[("dvs", t)], dma_q="dA")
        if t >= 1:
            drain_prep(per)

    nc.sbuf_base, nc.sbuf_top = sb_savedB
    Wq_sb = SB("Wq_sb", [128, EC, H * 128], BF16)
    etb = SB("etb", [128, NT * 512], BF16)
    v_sb = [SB(f"v_sb{i}", [128, NT * 128], BF16) for i in range(2)]
    Qg = SB("Qg", [128, G, 512], BF16)
    x1sb = SB("x1sb", [128, E], F32)
    msk = [SB(f"msk{i}", [128, 1024], BF16) for i in range(2)]
    rden = SB("rden", [128, 16], F32)

    psum_reset()
    s2 = [PS(f"s2_{i}", [128, 1024], F32) for i in range(2)]
    o_ps = [PS(f"o_ps{i}", [128, 512], F32) for i in range(2)]
    den_ps = PS("den_ps", [128, 512], F32)
    K_s2 = [KEY_B[0:2], KEY_B[2:4]]
    K_ops = [[KEY_B[4]], [KEY_B[5]]]
    K_den = [KEY_B[6]]
    psum_reset()
    xtr2 = PS("xtr2", [128, 2048], BF16)
    K_xtr2 = KEY_B[0:2]
    psum_reset()
    _dummy2 = PS("dummy2", [128, 1024], F32)
    q_ps = [PS(f"q_ps{g}", [128, 512], F32) for g in range(G)]
    K_qps = [[KEY_B[2 + g]] for g in range(G)]

    for ech in range(EC):
        extra = [("ktm",), ("vtl", 0), ("vtl", 1), ("bvb",)] if ech == 0 else []
        p.op("G", lambda g, c=ech: g.dma_start(out=Wq_sb[:, c, :], in_=d_wq[c]),
             writes=[("wq", ech), ("wkv", ech)] + extra, dma_q="dG")

    def ph2_head(j):
        b = j % 2
        p.op("Y", lambda y, j=j, b=b: y.dma_start(out=xst[b][:], in_=d_xo[j]),
             writes=[("xst", b)], dma_q="dY")
        p.op("Y", lambda y, j=j, b=b: y.dma_start(out=msk[b][:], in_=d_mask[j]),
             writes=[("msk", b)], dma_q="dY")
        ln_chain(("xst", b), xst[b][:], 1, xln_tm[b][:], ("xln_tm", b))

    ph2_head(0)
    for j in range(NO):
        b = j % 2
        nkt = 2 * j + 2
        transpose16(xln_tm[b][:], ("xln_tm", b), xtr2, K_xtr2, xln_fm[b][:],
                    ("xln_fm", b))

        def q_mms(te, b=b):
            out = []
            for c in range(EC):
                for h in range(H):
                    out.append(te.matmul(
                        q_ps[h // 4][:, 128 * (h % 4):128 * (h % 4) + 128],
                        Wq_sb[:, c, 128 * h:128 * h + 128],
                        xln_fm[b][:, 128 * c:128 * c + 128],
                        start=(c == 0 and h % 4 == 0),
                        stop=(c == EC - 1 and h % 4 == 3)))
            return out
        p.op("T", q_mms, reads=[("xln_fm", b)] + [("wq", c) for c in range(EC)],
             writes=sum(K_qps, []))
        p.op("S", lambda s: [s.activation(
                Qg[:, h // 4, 128 * (h % 4):128 * (h % 4) + 128],
                q_ps[h // 4][:, 128 * (h % 4):128 * (h % 4) + 128],
                AF.Identity, bias=bqs[:, h:h + 1], scale=RSQ) for h in range(H)],
             reads=sum(K_qps, []) + [("bqs",)], writes=[("Qg",)])
        if j + 1 < NO:
            ph2_head(j + 1)

        for g in range(G):
            vb = g % 2
            ob = g % 2
            p.op("Y", lambda y, g=g, vb=vb, nkt=nkt: y.dma_start(
                    out=v_sb[vb][:, 0:128 * nkt].rearrange(
                        "p (n d) -> p n d", d=128),
                    in_=d_vs[0:nkt, :, 128 * g:128 * g + 128].transpose([1, 0, 2])),
                 reads=[("dvs", kt) for kt in range(nkt)],
                 writes=[("vsb", vb)], dma_q="dY")

            def pv(te, kt, g=g, vb=vb, ob=ob, nkt=nkt):
                out = []
                st = kt == 0
                sp = kt == nkt - 1
                for h in range(4):
                    out.append(te.matmul(
                        o_ps[ob][:, 128 * h:128 * h + 128],
                        etb[:, 512 * kt + 128 * h:512 * kt + 128 * h + 128],
                        v_sb[vb][:, 128 * kt:128 * kt + 128],
                        start=(st and h == 0), stop=(sp and h == 3)))
                    out.append(te.matmul(
                        den_ps[:, 4 * g + h:4 * g + h + 1],
                        etb[:, 512 * kt + 128 * h:512 * kt + 128 * h + 128],
                        onesb[:, 0:1],
                        start=(st and h == 0), stop=(sp and h == 3)))
                return out

            def emit_pv(kt, pv=pv, vb=vb, ob=ob):
                p.op("T", lambda te, kt=kt, pv=pv: pv(te, kt),
                     reads=[("et", kt), ("vsb", vb), ("onesb",)],
                     writes=K_ops[ob] + K_den, waw_free=True)

            npair = nkt // 2
            for pp in range(npair):
                k0 = 2 * pp
                s2b = pp % 2
                for half in range(2):
                    p.op("T", lambda te, g=g, kt=k0 + half, s2b=s2b,
                         half=half: te.matmul(
                            s2[s2b][:, 512 * half:512 * half + 512],
                            Kd[g][:, 128 * kt:128 * kt + 128],
                            Qg[:, g, :], start=True, stop=True),
                         reads=[("Kd", k0 + half), ("Qg",)],
                         writes=[K_s2[s2b][half]])
                p.op("S", lambda s, k0=k0, s2b=s2b: s.activation(
                        etb[:, 512 * k0:512 * k0 + 1024], s2[s2b][:],
                        AF.Exp, bias=c127[:, 3:4]),
                     reads=K_s2[s2b] + [("c127",)],
                     writes=[("et", k0), ("et", k0 + 1)])
                if pp == npair - 1:
                    for half in range(2):
                        kt = k0 + half
                        p.op("V", lambda v, kt=kt, w=half, b=b: v.tensor_tensor(
                                etb[:, 512 * kt:512 * kt + 512],
                                etb[:, 512 * kt:512 * kt + 512],
                                msk[b][:, 512 * w:512 * w + 512], ALU.mult),
                             reads=[("et", kt), ("msk", b)], writes=[("et", kt)])
                if pp >= 1:
                    emit_pv(k0 - 2)
                    emit_pv(k0 - 1)
            emit_pv(nkt - 2)
            emit_pv(nkt - 1)

            p.op("V", lambda v, g=g: v.reciprocal(rden[:, 4 * g:4 * g + 4],
                                                  den_ps[:, 4 * g:4 * g + 4]),
                 reads=K_den, writes=[("rden", g)])
            p.op("V", lambda v, g=g, ob=ob, b=b: [v.scalar_tensor_tensor(
                    x1sb[:, 512 * g + 128 * h:512 * g + 128 * h + 128],
                    o_ps[ob][:, 128 * h:128 * h + 128],
                    rden[:, 4 * g + h:4 * g + h + 1],
                    xst[b][:, 512 * g + 128 * h:512 * g + 128 * h + 128],
                    ALU.mult, ALU.add) for h in range(4)],
                 reads=K_ops[ob] + [("rden", g), ("xst", b)],
                 writes=[("x1sb", g)])
        p.op("S", lambda g, j=j: g.dma_start(out=d_x1[j], in_=x1sb[:]),
             reads=[("x1sb", gg) for gg in range(G)],
             writes=[("dx1", j)], dma_q="dA")
        drain_prep(per)
    drain_prep(n_prep)

    p.barrier()
    nc.sbuf_base, nc.sbuf_top = sb_savedA
    identf = SB("sb_identf", [128, 128], F32)
    x1f = SB("x1f", [128, DT * NTOK], F32)
    xqf = SB("xqf", [128, EC * NTOK], BF16)
    t1 = SB("t1", [128, E], F32)
    i32 = SB("i32", [128, 512], I32)
    xq_tm = SB("xq_tm", [128, E], BF16)
    pgrp = SB("pgrp", [128, HT * NTOK], BF16)
    swg_t = [SB(f"swg{i}", [128, E], BF16) for i in range(2)]
    swu_t = [SB(f"swu{i}", [128, E], BF16) for i in range(2)]
    swd_t = [SB(f"swd{i}", [128, FH], BF16) for i in range(2)]
    g_sb = SB("g_sb", [128, 512], BF16)
    out_sb = [SB(f"out_sb{i}", [128, NTOK], F32) for i in range(2)]

    psum_reset()
    x1tr = PS("x1tr", [128, 512], F32)
    xqtr = PS("xqtr", [128, 2048], BF16)
    g_ps = [PS(f"g_ps{i}", [128, 512], F32) for i in range(2)]
    u_ps = [PS(f"u_ps{i}", [128, 512], F32) for i in range(2)]
    d_ps0 = PS("d_ps0", [128, 512], F32)
    K_x1tr = [KEY_B[0]]
    K_xqtr = KEY_B[1:3]
    K_gps = [[KEY_B[3]], [KEY_B[4]]]
    K_ups = [[KEY_B[5]], [KEY_B[6]]]
    d_ps = [d_ps0, x1tr]
    K_dps = [[KEY_B[7]], [KEY_B[0]]]

    p.op("Y", lambda y: y.dma_start(out=identf[:], in_=d_identf),
         writes=[("identf",)], dma_q="dY")
    x1f3 = x1f[:].rearrange("p (d t) -> p d t", t=NTOK)
    xqf3 = xqf[:].rearrange("p (c t) -> p c t", t=NTOK)

    for grp in range(NG):
        for jj in range(GT):
            j4 = GT * grp + jj
            b = jj % 2
            p.op("Y", lambda y, j4=j4, b=b: y.dma_start(out=xst[b][:],
                                                        in_=d_x1[j4]),
                 reads=[("dx1", j4)], writes=[("xst", b)], dma_q="dY")
            for r in range(4):
                p.op("T", lambda t, b=b, r=r: [t.transpose(
                        x1tr[:, 128 * c:128 * c + 128],
                        xst[b][:, 512 * r + 128 * c:512 * r + 128 * c + 128],
                        identf[:]) for c in range(4)],
                     reads=[("xst", b), ("identf",)], writes=K_x1tr)
                p.op("V", lambda v, r=r, jj=jj: v.tensor_copy(
                        x1f3[:, 4 * r:4 * r + 4, 128 * jj:128 * jj + 128],
                        x1tr[:, 0:512].rearrange("p (a q) -> p a q", q=128)),
                     reads=K_x1tr, writes=[("x1f", jj, r)])
            ln_chain_s(("xst", b), xst[b][:], 1, t1[:], ("t1",))
            ln_chain_s(("t1",), t1[:], 2, t1[:], ("t1",))
            p.op("V", lambda v: v.reduce_max(st2[:, 4:5], t1[:], axis=AX,
                                             apply_absolute_value=True),
                 reads=[("t1",)], writes=[("am",)])
            p.op("V", lambda v: v.reciprocal(st2[:, 7:8], st2[:, 4:5]),
                 reads=[("am",)], writes=[("ram",)])
            p.op("V", lambda v: v.tensor_scalar(st2[:, 5:6], st2[:, 7:8],
                                                127.0, None, ALU.mult),
                 reads=[("ram",)], writes=[("s",)])
            p.op("V", lambda v: v.tensor_scalar(st2[:, 6:7], st2[:, 4:5],
                                                float(1.0 / 127.0), None,
                                                ALU.mult),
                 reads=[("am",)], writes=[("rs",)])
            for c in range(4):
                sl = slice(512 * c, 512 * c + 512)
                p.op("V", lambda v, sl=sl: v.tensor_scalar(
                        t1[:, sl], t1[:, sl], st2[:, 5:6], MAGIC,
                        ALU.mult, ALU.add),
                     reads=[("t1",), ("s",)], writes=[("t1",)])
                p.op("V", lambda v, sl=sl: v.tensor_scalar(
                        i32[:], t1[:, sl], MAGIC, None, ALU.subtract),
                     reads=[("t1",)], writes=[("i32",)])
                p.op("V", lambda v, sl=sl: v.tensor_scalar(
                        xq_tm[:, sl], i32[:], st2[:, 6:7], None, ALU.mult),
                     reads=[("i32",), ("rs",)], writes=[("xq_tm",)])
            p.op("T", lambda t: [t.transpose(xqtr[:, 128 * c:128 * c + 128],
                                             xq_tm[:, 128 * c:128 * c + 128],
                                             identb[:]) for c in range(EC)],
                 reads=[("xq_tm",), ("identb",)], writes=K_xqtr)
            p.op("V", lambda v, jj=jj: v.tensor_copy(
                    xqf3[:, :, 128 * jj:128 * jj + 128],
                    xqtr[:, 0:128 * EC].rearrange("p (c q) -> p c q", q=128)),
                 reads=K_xqtr, writes=[("xqf", jj)])

        xq_keys = [("xqf", jj) for jj in range(GT)]
        x1_keys = [("x1f", jj, r) for jj in range(GT) for r in range(4)]
        for ht in range(HT):
            hb = ht % 2
            p.op("S", lambda s, ht=ht, hb=hb: s.dma_start(
                    out=swg_t[hb][:], in_=d_sg[ht]),
                 reads=[("dsg", ht)], writes=[("swg", hb)], dma_q="dA")
            p.op("S", lambda s, ht=ht, hb=hb: s.dma_start(
                    out=swu_t[hb][:], in_=d_su[ht]),
                 reads=[("dsu", ht)], writes=[("swu", hb)], dma_q="dA")

            def gu(te, wt, pp):
                return [te.matmul(pp[:, 0:NTOK], wt[:, 128 * c:128 * c + 128],
                                  xqf[:, NTOK * c:NTOK * c + NTOK],
                                  start=(c == 0), stop=(c == EC - 1))
                        for c in range(EC)]
            p.op("T", lambda te, hb=hb: gu(te, swg_t[hb], g_ps[hb]),
                 reads=[("swg", hb)] + xq_keys, writes=K_gps[hb])
            p.op("T", lambda te, hb=hb: gu(te, swu_t[hb], u_ps[hb]),
                 reads=[("swu", hb)] + xq_keys, writes=K_ups[hb])
            p.op("S", lambda s, hb=hb: s.activation(
                    g_sb[:, 0:NTOK], g_ps[hb][:, 0:NTOK], AF.Silu),
                 reads=K_gps[hb], writes=[("g_sb",)])
            p.op("V", lambda v, ht=ht, hb=hb: v.tensor_tensor(
                    pgrp[:, NTOK * ht:NTOK * ht + NTOK],
                    u_ps[hb][:, 0:NTOK], g_sb[:, 0:NTOK], ALU.mult),
                 reads=K_ups[hb] + [("g_sb",)], writes=[("pg", ht)])

        pg_keys = [("pg", ht) for ht in range(HT)]
        for dt in range(DT):
            db = dt % 2
            p.op("S", lambda s, dt=dt, db=db: s.dma_start(
                    out=swd_t[db][:], in_=d_sd[dt]),
                 reads=[("dsd", dt, h) for h in range(max(1, FH // ncol_d))],
                 writes=[("swd", db)], dma_q="dA")
            p.op("T", lambda te, dt=dt, db=db: [te.matmul(
                    d_ps[db][:, 0:NTOK], swd_t[db][:, 128 * hc:128 * hc + 128],
                    pgrp[:, NTOK * hc:NTOK * hc + NTOK],
                    start=(hc == 0), stop=(hc == HT - 1)) for hc in range(HT)],
                 reads=[("swd", db)] + pg_keys, writes=K_dps[db])
            p.op("V", lambda v, dt=dt, db=db: v.tensor_tensor(
                    out_sb[db][:], d_ps[db][:, 0:NTOK],
                    x1f[:, NTOK * dt:NTOK * dt + NTOK], ALU.add),
                 reads=K_dps[db] + x1_keys, writes=[("osb", db)])
            p.op("Y", lambda g, grp=grp, dt=dt, db=db: g.dma_start(
                    out=d_out[grp, dt], in_=out_sb[db][:]),
                 reads=[("osb", db)], dma_q="dY")

    p.finalize()
    return nc


def _prep_inputs(cfg, x_b, par, Wq, bq, Wk, bk, Wv, bv, Wg, Wu, Wd, masks):
    NT, EC, H, G, E = cfg.NT, cfg.EC, cfg.H, cfg.G, cfg.E
    m = {}
    m["x_full"] = np.ascontiguousarray(x_b)
    m["x_own"] = np.ascontiguousarray(x_b.reshape(NT, 128, E)[par::2])
    m["wq_in"] = np.ascontiguousarray(
        Wq.reshape(H, EC, 128, 128).transpose(1, 2, 0, 3)
          .reshape(EC, 128, H * 128))
    wkv = np.concatenate([Wk, Wv], axis=0)
    m["wkv_in"] = np.ascontiguousarray(
        wkv.reshape(2 * G, EC, 128, 128).transpose(1, 2, 0, 3)
           .reshape(EC, 128, 2 * G * 128))
    m["bqT"] = np.ascontiguousarray(bq.T)
    m["bkT"] = np.ascontiguousarray(bk.T)
    m["bv_bcast"] = np.ascontiguousarray(
        np.broadcast_to(bv.reshape(1, G * 128), (128, G * 128)))
    m["w_g"] = np.ascontiguousarray(Wg)
    m["w_u"] = np.ascontiguousarray(Wu)
    m["w_d"] = np.ascontiguousarray(Wd)
    m["mask_tab"] = masks
    m["identb"] = np.eye(128, dtype=BF16NP)
    m["identf"] = np.eye(128, dtype=np.float32)
    m["onesb"] = np.ones((128, 8), dtype=BF16NP)
    cc = np.zeros((128, 8), dtype=np.float32)
    cc[:, 0] = 127.0
    cc[:, 1] = EPS1
    cc[:, 2] = EPS2
    cc[:, 3] = -EXPC
    m["c127"] = cc
    return m


def _masks(cfg, par):
    NO = cfg.NO
    tab = np.zeros((NO, 128, 2, 4, 128), dtype=BF16NP)
    kp = np.arange(128)[:, None]
    qf = np.arange(128)[None, :]
    tri = (kp <= qf).astype(BF16NP)
    for j in range(NO):
        if par == 0:
            tab[j, :, 0] = tri[:, None, :]
            tab[j, :, 1] = 0.0
        else:
            tab[j, :, 0] = 1.0
            tab[j, :, 1] = tri[:, None, :]
    return np.ascontiguousarray(tab.reshape(NO, 128, 1024))


def kernel(L0_attn_x, L0_attn_Wq, L0_attn_bq, L0_attn_Wk, L0_attn_bk,
           L0_attn_Wv, L0_attn_bv, L0_ffn_W_g, L0_ffn_W_u, L0_ffn_W_d):
    cfg = Cfg()
    x = np.asarray(L0_attn_x, dtype=np.float32)
    B, S, E = x.shape
    nc = build_program(cfg)
    args = [np.asarray(a, np.float32) for a in
            (L0_attn_Wq, L0_attn_bq, L0_attn_Wk, L0_attn_bk,
             L0_attn_Wv, L0_attn_bv, L0_ffn_W_g, L0_ffn_W_u, L0_ffn_W_d)]
    in_maps = []
    for c in range(8):
        b, par = c // 2, c % 2
        in_maps.append(_prep_inputs(cfg, x[b], par, *args, _masks(cfg, par)))
    res = run_bass_kernel_spmd(nc, in_maps, core_ids=list(range(8)))
    out = np.empty((B, S, E), dtype=np.float32)
    NO, DT, GT, NG = cfg.NO, cfg.DT, cfg.GT, cfg.NG
    for c in range(8):
        b, par = c // 2, c % 2
        o = np.asarray(res.results[c]["out"])
        o = o.reshape(NG, DT, 128, GT, 128).transpose(0, 3, 4, 1, 2)
        o = o.reshape(NO, 128, DT * 128)
        out[b].reshape(cfg.NT, 128, E)[par::2] = o
    return out


# version 17
# speedup vs baseline: 1.0014x; 1.0014x over previous
import numpy as np
import ml_dtypes
import concourse.bass as bass
import concourse.mybir as mybir
from concourse.bass_utils import run_bass_kernel_spmd

F32 = mybir.dt.float32
BF16 = mybir.dt.bfloat16
I32 = mybir.dt.int32
AX = mybir.AxisListType.X
AF = mybir.ActivationFunctionType
ALU = mybir.AluOpType
BF16NP = ml_dtypes.bfloat16

MAGIC = float(1.5 * 2 ** 23)
EXPC = 16.0
RSQ = float(1.0 / np.sqrt(128.0))
EPS1 = 1e-5
EPS2 = 1e-8


class Cfg:
    def __init__(self, NT=32, HT=64, GT=4):
        self.NT = NT
        self.NO = NT // 2
        self.EC = 16
        self.H = 16
        self.G = 4
        self.HT = HT
        self.DT = 16
        self.GT = GT
        self.NG = self.NO // GT
        self.E = self.EC * 128
        self.FH = HT * 128


class Op:
    __slots__ = ("eng", "emit", "deps", "need_inc", "sem_val", "dma_q", "idx")

    def __init__(self, eng, emit, dma_q, idx):
        self.eng = eng
        self.emit = emit
        self.deps = []
        self.need_inc = dma_q is not None
        self.sem_val = None
        self.dma_q = dma_q
        self.idx = idx


class Prog:

    ENGS = ("Y", "S", "V", "T", "G")
    NPOOL = 8

    def __init__(self, nc):
        self.nc = nc
        self.ops = []
        self.writer = {}
        self.readers = {}
        self.last_on = {}
        self.barrier_deps = None
        self.barrier_seen = set()
        self.qidx = {"dY": 0, "dA": 0, "dG": 0}
        self.counts = {}
        self.throttle = {}

    def barrier(self):
        self.barrier_deps = [o for o in self.last_on.values()]
        self.barrier_seen = set()

    def op(self, eng, emit, reads=(), writes=(), waw_free=False, dma_q=None):
        rec = Op(eng, emit, dma_q, len(self.ops))
        lane = eng
        if dma_q is not None:
            k = self.qidx[dma_q]
            self.qidx[dma_q] = k + 1
            slot = (dma_q, k % self.NPOOL)
            v = self.counts.get(slot, 0) + 16
            self.counts[slot] = v
            rec.sem_val = (slot, v)
            if v > 16:
                self.throttle[rec.idx] = (slot, v - 16)
            lane = slot
        deps = {}

        def dep(pr):
            if pr is not None and pr is not rec:
                deps[pr.idx] = pr

        if self.barrier_deps is not None:
            if lane not in self.barrier_seen:
                self.barrier_seen.add(lane)
                for pr in self.barrier_deps:
                    dep(pr)
        for k in reads:
            dep(self.writer.get(k))
        for k in writes:
            w = self.writer.get(k)
            if w is not None and not (waw_free and w.eng == "T" and eng == "T"
                                      and w.dma_q is None and dma_q is None):
                dep(w)
            for r in self.readers.get(k, ()):
                dep(r)
        for k in reads:
            self.readers.setdefault(k, []).append(rec)
        for k in writes:
            self.writer[k] = rec
            self.readers[k] = []
        for pr in deps.values():
            pr.need_inc = True
            rec.deps.append(pr)
        self.ops.append(rec)
        self.last_on[lane] = rec
        return rec

    def finalize(self):
        nc = self.nc
        sems = {}
        for e in self.ENGS:
            sems[e] = nc.alloc_semaphore(f"sem_{e}")
        for q in ("dY", "dA", "dG"):
            for i in range(self.NPOOL):
                sems[(q, i)] = nc.alloc_semaphore(f"sem_{q}{i}")
        counts = dict.fromkeys(sems, 0)
        counts.update(self.counts)
        throttle = self.throttle
        for rec in self.ops:
            if rec.dma_q is None and rec.need_inc:
                counts[rec.eng] += 1
                rec.sem_val = (rec.eng, counts[rec.eng])
        totals = dict(counts)
        by_eng = {e: [] for e in self.ENGS}
        for rec in self.ops:
            by_eng[rec.eng].append(rec)

        with nc.Block() as block:
            def emit_engine(ename, engobj):
                water = dict.fromkeys(sems, 0)
                for rec in by_eng[ename]:
                    for pr in rec.deps:
                        s, v = pr.sem_val
                        if water[s] < v:
                            engobj.wait_ge(sems[s], v)
                            water[s] = v
                    th = throttle.get(rec.idx)
                    if th is not None and water[th[0]] < th[1]:
                        engobj.wait_ge(sems[th[0]], th[1])
                        water[th[0]] = th[1]
                    insts = rec.emit(engobj)
                    if rec.sem_val is not None:
                        s, v = rec.sem_val
                        last = insts[-1] if isinstance(insts, (list, tuple)) else insts
                        last.then_inc(sems[s], 16 if rec.dma_q else 1)
                for q, owner in (("dY", "Y"), ("dA", "S"), ("dG", "G")):
                    if owner == ename:
                        for i in range(self.NPOOL):
                            if totals[(q, i)] > 0 and water[(q, i)] < totals[(q, i)]:
                                engobj.wait_ge(sems[(q, i)], totals[(q, i)])

            for ename, secname in (("Y", "sync"), ("S", "scalar"),
                                   ("V", "vector"), ("T", "tensor"),
                                   ("G", "gpsimd")):
                getattr(block, secname)(
                    lambda eo, _n=ename: emit_engine(_n, eo))
        return totals


def build_program(cfg: Cfg):
    nc = bass.Bass("TRN2", target_bir_lowering=False)
    NT, NO, EC, H, G, HT, DT, GT, NG = (cfg.NT, cfg.NO, cfg.EC, cfg.H,
                                        cfg.G, cfg.HT, cfg.DT, cfg.GT, cfg.NG)
    E, FH = cfg.E, cfg.FH
    NTOK = GT * 128

    d_x = nc.dram_tensor("x_full", [NT * 128, E], F32, kind="ExternalInput").ap()
    d_xo = nc.dram_tensor("x_own", [NO, 128, E], F32, kind="ExternalInput").ap()
    d_wq = nc.dram_tensor("wq_in", [EC, 128, H * 128], F32, kind="ExternalInput").ap()
    d_wkv = nc.dram_tensor("wkv_in", [EC, 128, 2 * G * 128], F32, kind="ExternalInput").ap()
    d_bq = nc.dram_tensor("bqT", [128, H], F32, kind="ExternalInput").ap()
    d_bk = nc.dram_tensor("bkT", [128, G], F32, kind="ExternalInput").ap()
    d_bv = nc.dram_tensor("bv_bcast", [128, G * 128], F32, kind="ExternalInput").ap()
    d_wg = nc.dram_tensor("w_g", [FH, E], F32, kind="ExternalInput").ap()
    d_wu = nc.dram_tensor("w_u", [FH, E], F32, kind="ExternalInput").ap()
    d_wd = nc.dram_tensor("w_d", [E, FH], F32, kind="ExternalInput").ap()
    d_mask = nc.dram_tensor("mask_tab", [NO, 128, 1024], BF16, kind="ExternalInput").ap()
    d_identb = nc.dram_tensor("identb", [128, 128], BF16, kind="ExternalInput").ap()
    d_identf = nc.dram_tensor("identf", [128, 128], F32, kind="ExternalInput").ap()
    d_onesb = nc.dram_tensor("onesb", [128, 8], BF16, kind="ExternalInput").ap()
    d_c127 = nc.dram_tensor("c127", [128, 8], F32, kind="ExternalInput").ap()

    d_vs = nc.dram_tensor("v_scr", [NT, 128, G * 128], BF16).ap()
    d_wgb = nc.dram_tensor("wg_b16", [FH, E], BF16).ap()
    d_wub = nc.dram_tensor("wu_b16", [FH, E], BF16).ap()
    d_wdb = nc.dram_tensor("wd_b16", [E, FH], BF16).ap()
    d_x1 = nc.dram_tensor("x1_scr", [NO, 128, E], F32).ap()
    d_sg = nc.dram_tensor("swg_scr", [HT, 128, E], BF16).ap()
    d_su = nc.dram_tensor("swu_scr", [HT, 128, E], BF16).ap()
    d_sd = nc.dram_tensor("swd_scr", [DT, 128, FH], BF16).ap()
    d_out = nc.dram_tensor("out", [NG, DT, 128, NTOK], F32, kind="ExternalOutput").ap()

    p = Prog(nc)
    SB = nc.alloc_sbuf_tensor
    PS = nc.alloc_psum_tensor

    identb = SB("sb_identb", [128, 128], BF16)
    onesb = SB("sb_onesb", [128, 8], BF16)
    c127 = SB("sb_c127", [128, 8], F32)
    bqs = SB("sb_bqs", [128, H], F32)
    bkT = SB("sb_bkT", [128, G], F32)
    xst = [SB(f"xst{i}", [128, E], F32) for i in range(2)]
    stats = SB("stats", [128, 32], F32)
    st2 = SB("st2", [128, 12], F32)

    KEY_B = [("PB", i) for i in range(8)]
    psum0 = (nc.psum_base, nc.psum_top)

    def psum_reset():
        nc.psum_base, nc.psum_top = psum0

    kvA = PS("kvA", [128, 1024], F32)
    kvB = PS("kvB", [128, 1024], F32)
    xtr1 = PS("xtr1", [128, 2048], BF16)
    ktr1 = PS("ktr1", [128, 1024], BF16)
    wtr = PS("wtr", [128, 1024], BF16)
    K_kvA, K_kvB = KEY_B[0:2], KEY_B[2:4]
    K_xtr1, K_ktr1, K_wtr = KEY_B[4:6], [KEY_B[6]], [KEY_B[7]]

    sb_savedA = (nc.sbuf_base, nc.sbuf_top)
    xln_tm = [SB(f"xln_tm{i}", [128, E], BF16) for i in range(2)]
    xln_fm = [SB(f"xln_fm{i}", [128, E], BF16) for i in range(2)]
    Kd = [SB(f"Kd{g}", [128, NT * 128], BF16) for g in range(G)]
    wst = [SB(f"wst{i}", [128, E], BF16) for i in range(2)]
    wtg = SB("wtg0", [128, E], BF16)
    sb_savedB = (nc.sbuf_base, nc.sbuf_top)
    Wkv_sb = SB("Wkv_sb", [128, EC, 2 * G * 128], BF16)
    ktm = SB("ktm", [128, G * 128], BF16)
    vtl = [SB(f"vtl{i}", [128, G * 128], BF16) for i in range(2)]
    bvb = SB("sb_bvb", [128, G * 128], F32)

    def ln_chain(src_key, src_ap, eps_col, out_ap, out_key):
        nchunk = E // 512
        p.op("V", lambda v: [v.bn_stats(stats[:, 6 * c:6 * c + 6],
                                        src_ap[:, 512 * c:512 * c + 512])
                             for c in range(nchunk)],
             reads=[src_key], writes=[("bn",)])
        p.op("V", lambda v: v.bn_aggr(st2[:, 0:2], stats[:, 0:6 * nchunk]),
             reads=[("bn",)], writes=[("mv",)])
        p.op("S", lambda s: s.activation(st2[:, 2:3], st2[:, 1:2], AF.Sqrt,
                                         bias=c127[:, eps_col:eps_col + 1]),
             reads=[("mv",), ("c127",)], writes=[("std",)])
        p.op("V", lambda v: v.reciprocal(st2[:, 3:4], st2[:, 2:3]),
             reads=[("std",)], writes=[("rstd",)])
        p.op("V", lambda v: v.tensor_scalar(out_ap, src_ap, st2[:, 0:1],
                                            st2[:, 3:4], ALU.subtract,
                                            ALU.mult),
             reads=[src_key, ("mv",), ("rstd",)], writes=[out_key])

    def ln_chain_s(src_key, src_ap, eps_col, out_ap, out_key):
        nchunk = E // 512
        p.op("V", lambda v: [v.bn_stats(stats[:, 6 * c:6 * c + 6],
                                        src_ap[:, 512 * c:512 * c + 512])
                             for c in range(nchunk)],
             reads=[src_key], writes=[("bn",)])
        p.op("V", lambda v: v.bn_aggr(st2[:, 0:2], stats[:, 0:6 * nchunk]),
             reads=[("bn",)], writes=[("mv",)])
        p.op("S", lambda s: s.activation(st2[:, 2:3], st2[:, 1:2], AF.Sqrt,
                                         bias=c127[:, eps_col:eps_col + 1]),
             reads=[("mv",), ("c127",)], writes=[("std",)])
        p.op("V", lambda v: v.reciprocal(st2[:, 3:4], st2[:, 2:3]),
             reads=[("std",)], writes=[("rstd",)])
        p.op("V", lambda v: v.tensor_scalar(st2[:, 8:9], st2[:, 0:1],
                                            st2[:, 3:4], -1.0,
                                            ALU.mult, ALU.mult),
             reads=[("mv",), ("rstd",)], writes=[("negmr",)])
        p.op("S", lambda s: s.activation(out_ap, src_ap, AF.Identity,
                                         bias=st2[:, 8:9], scale=st2[:, 3:4]),
             reads=[src_key, ("negmr",), ("rstd",)], writes=[out_key])

    def transpose16(src_ap, src_key, ps, ps_keys, dst_ap, dst_key):
        p.op("T", lambda t: [t.transpose(ps[:, 128 * c:128 * c + 128],
                                         src_ap[:, 128 * c:128 * c + 128],
                                         identb[:]) for c in range(EC)],
             reads=[src_key, ("identb",)], writes=ps_keys)
        p.op("V", lambda v: v.tensor_copy(dst_ap, ps[:, 0:128 * EC]),
             reads=ps_keys, writes=[dst_key])

    for sb_t, key, src in ((identb, "identb", d_identb),
                           (onesb, "onesb", d_onesb), (c127, "c127", d_c127),
                           (bqs, "bqs", d_bq), (bkT, "bkT", d_bk),
                           (bvb, "bvb", d_bv)):
        p.op("Y", lambda y, sb_t=sb_t, src=src: y.dma_start(out=sb_t[:], in_=src),
             writes=[(key,)], dma_q="dY")
    p.op("S", lambda s: s.mul(bqs[:], bqs[:], RSQ),
         reads=[("bqs",)], writes=[("bqs",)])
    for ech in range(EC):
        p.op("G", lambda g, c=ech: g.dma_start(out=Wkv_sb[:, c, :], in_=d_wkv[c]),
             writes=[("wkv", ech)], dma_q="dG")

    CCH = max(128, FH // 8)
    cast_plan = []
    for src, dst, rows, key in ((d_wg, d_wgb, FH, "cwg"), (d_wu, d_wub, FH, "cwu")):
        for r0 in range(0, rows, CCH):
            cast_plan.append((src[r0:r0 + CCH, :], dst[r0:r0 + CCH, :],
                              (key, r0 // CCH)))
    CCD = max(128, E // 8)
    for r0 in range(0, E, CCD):
        cast_plan.append((d_wd[r0:r0 + CCD, :], d_wdb[r0:r0 + CCD, :],
                          ("cwd", r0 // CCD)))
    for src_ap, dst_ap, ckey in cast_plan:
        p.op("G", lambda g, s_=src_ap, d_=dst_ap: g.dma_start(out=d_, in_=s_),
             writes=[ckey], dma_q="dG")

    prep_srcs = []

    def prep_issue(k):
        src_ap, scr_ap, ncol, scr_key, cast_key = prep_srcs[k]
        wb = k % 2
        p.op("S", lambda g, wb=wb, ncol=ncol, src_ap=src_ap: g.dma_start(
                out=wst[wb][:, 0:ncol], in_=src_ap),
             reads=[cast_key], writes=[("wst", wb)], dma_q="dA")

    def prep_compute(k):
        src_ap, scr_ap, ncol, scr_key, cast_key = prep_srcs[k]
        wb = k % 2
        nb = ncol // 128
        for h0 in range(0, nb, 8):
            hn = min(8, nb - h0)
            p.op("T", lambda t, h0=h0, hn=hn, wb=wb: [t.transpose(
                    wtr[:, 128 * c:128 * c + 128],
                    wst[wb][:, 128 * (h0 + c):128 * (h0 + c) + 128],
                    identb[:]) for c in range(hn)],
                 reads=[("wst", wb), ("identb",)], writes=K_wtr)
            p.op("S", lambda s, h0=h0, hn=hn: s.activation(
                    wtg[:, 128 * h0:128 * (h0 + hn)], wtr[:, 0:128 * hn],
                    AF.Sign),
                 reads=K_wtr, writes=[("wtg",)])
        p.op("G", lambda g, ncol=ncol, scr_ap=scr_ap: g.dma_start(
                out=scr_ap, in_=wtg[:, 0:ncol]),
             reads=[("wtg",)], writes=[scr_key], dma_q="dG")

    for ht in range(HT):
        prep_srcs.append((d_wgb[128 * ht:128 * ht + 128, :], d_sg[ht], E,
                          ("dsg", ht), ("cwg", 128 * ht // CCH)))
        prep_srcs.append((d_wub[128 * ht:128 * ht + 128, :], d_su[ht], E,
                          ("dsu", ht), ("cwu", 128 * ht // CCH)))
    ncol_d = min(E, FH)
    for dt in range(DT):
        for hcg in range(max(1, FH // ncol_d)):
            prep_srcs.append((
                d_wdb[128 * dt:128 * dt + 128, ncol_d * hcg:ncol_d * (hcg + 1)],
                d_sd[dt][:, ncol_d * hcg:ncol_d * (hcg + 1)], ncol_d,
                ("dsd", dt, hcg), ("cwd", 128 * dt // CCD)))
    n_prep = len(prep_srcs)
    state = {"issued": 0, "done": 0}

    def drain_prep(k):
        while k > 0 and state["done"] < n_prep:
            while state["issued"] < min(n_prep, state["done"] + 2):
                prep_issue(state["issued"])
                state["issued"] += 1
            prep_compute(state["done"])
            state["done"] += 1
            k -= 1

    per = -(-n_prep // (NT + NO - 2)) if (NT + NO) > 2 else n_prep

    x_t = d_x.rearrange("(n p) e -> n p e", p=128)

    def ph1_head(t):
        b = t % 2
        p.op("Y", lambda y, t=t, b=b: y.dma_start(out=xst[b][:], in_=x_t[t]),
             writes=[("xst", b)], dma_q="dY")
        ln_chain(("xst", b), xst[b][:], 1, xln_tm[b][:], ("xln_tm", b))

    ph1_head(0)
    for t in range(NT):
        b = t % 2
        transpose16(xln_tm[b][:], ("xln_tm", b), xtr1, K_xtr1, xln_fm[b][:],
                    ("xln_fm", b))
        if t + 1 < NT:
            ph1_head(t + 1)
        kv = kvA if b == 0 else kvB
        K_kv = K_kvA if b == 0 else K_kvB

        def kv_mms(te, kv=kv, b=b):
            out = []
            for c in range(EC):
                for half in range(2):
                    out.append(te.matmul(
                        kv[:, 512 * half:512 * half + 512],
                        xln_fm[b][:, 128 * c:128 * c + 128],
                        Wkv_sb[:, c, 512 * half:512 * half + 512],
                        start=(c == 0), stop=(c == EC - 1)))
            return out
        p.op("T", kv_mms, reads=[("xln_fm", b)] + [("wkv", c) for c in range(EC)],
             writes=K_kv)
        p.op("S", lambda s, kv=kv: s.activation(ktm[:], kv[:, 0:512], AF.Copy),
             reads=K_kv, writes=[("ktm",)])
        p.op("T", lambda te: [te.transpose(ktr1[:, 128 * g:128 * g + 128],
                                           ktm[:, 128 * g:128 * g + 128],
                                           identb[:]) for g in range(G)],
             reads=[("ktm",), ("identb",)], writes=K_ktr1)
        p.op("S", lambda s, t=t: [s.activation(
                Kd[g][:, 128 * t:128 * t + 128],
                ktr1[:, 128 * g:128 * g + 128],
                AF.Identity, bias=bkT[:, g:g + 1]) for g in range(G)],
             reads=K_ktr1 + [("bkT",)], writes=[("Kd", t)])
        p.op("V", lambda v, kv=kv, b=b: v.tensor_tensor(vtl[b][:], kv[:, 512:1024],
                                                   bvb[:], ALU.add),
             reads=K_kv + [("bvb",)], writes=[("vtl", b)])
        p.op("S", lambda g, t=t, b=b: g.dma_start(out=d_vs[t], in_=vtl[b][:]),
             reads=[("vtl", b)], writes=[("dvs", t)], dma_q="dA")
        if t >= 1:
            drain_prep(per)

    nc.sbuf_base, nc.sbuf_top = sb_savedB
    Wq_sb = SB("Wq_sb", [128, EC, H * 128], BF16)
    etb = SB("etb", [128, NT * 512], BF16)
    v_sb = [SB(f"v_sb{i}", [128, NT * 128], BF16) for i in range(2)]
    Qg = SB("Qg", [128, G, 512], BF16)
    x1sb = SB("x1sb", [128, E], F32)
    msk = [SB(f"msk{i}", [128, 1024], BF16) for i in range(2)]
    rden = SB("rden", [128, 16], F32)

    psum_reset()
    s2 = [PS(f"s2_{i}", [128, 1024], F32) for i in range(2)]
    o_ps = [PS(f"o_ps{i}", [128, 512], F32) for i in range(2)]
    den_ps = PS("den_ps", [128, 512], F32)
    K_s2 = [KEY_B[0:2], KEY_B[2:4]]
    K_ops = [[KEY_B[4]], [KEY_B[5]]]
    K_den = [KEY_B[6]]
    psum_reset()
    xtr2 = PS("xtr2", [128, 2048], BF16)
    K_xtr2 = KEY_B[0:2]
    psum_reset()
    _dummy2 = PS("dummy2", [128, 1024], F32)
    q_ps = [PS(f"q_ps{g}", [128, 512], F32) for g in range(G)]
    K_qps = [[KEY_B[2 + g]] for g in range(G)]

    for ech in range(EC):
        extra = [("ktm",), ("vtl", 0), ("vtl", 1), ("bvb",)] if ech == 0 else []
        p.op("G", lambda g, c=ech: g.dma_start(out=Wq_sb[:, c, :], in_=d_wq[c]),
             writes=[("wq", ech), ("wkv", ech)] + extra, dma_q="dG")

    def ph2_head(j):
        b = j % 2
        p.op("Y", lambda y, j=j, b=b: y.dma_start(out=xst[b][:], in_=d_xo[j]),
             writes=[("xst", b)], dma_q="dY")
        p.op("Y", lambda y, j=j, b=b: y.dma_start(out=msk[b][:], in_=d_mask[j]),
             writes=[("msk", b)], dma_q="dY")
        ln_chain(("xst", b), xst[b][:], 1, xln_tm[b][:], ("xln_tm", b))

    ph2_head(0)
    for j in range(NO):
        b = j % 2
        nkt = 2 * j + 2
        transpose16(xln_tm[b][:], ("xln_tm", b), xtr2, K_xtr2, xln_fm[b][:],
                    ("xln_fm", b))

        def q_mms(te, b=b):
            out = []
            for c in range(EC):
                for h in range(H):
                    out.append(te.matmul(
                        q_ps[h // 4][:, 128 * (h % 4):128 * (h % 4) + 128],
                        Wq_sb[:, c, 128 * h:128 * h + 128],
                        xln_fm[b][:, 128 * c:128 * c + 128],
                        start=(c == 0 and h % 4 == 0),
                        stop=(c == EC - 1 and h % 4 == 3)))
            return out
        p.op("T", q_mms, reads=[("xln_fm", b)] + [("wq", c) for c in range(EC)],
             writes=sum(K_qps, []))
        p.op("S", lambda s: [s.activation(
                Qg[:, h // 4, 128 * (h % 4):128 * (h % 4) + 128],
                q_ps[h // 4][:, 128 * (h % 4):128 * (h % 4) + 128],
                AF.Identity, bias=bqs[:, h:h + 1], scale=RSQ) for h in range(H)],
             reads=sum(K_qps, []) + [("bqs",)], writes=[("Qg",)])
        if j + 1 < NO:
            ph2_head(j + 1)

        for g in range(G):
            vb = g % 2
            ob = g % 2
            p.op("Y", lambda y, g=g, vb=vb, nkt=nkt: y.dma_start(
                    out=v_sb[vb][:, 0:128 * nkt].rearrange(
                        "p (n d) -> p n d", d=128),
                    in_=d_vs[0:nkt, :, 128 * g:128 * g + 128].transpose([1, 0, 2])),
                 reads=[("dvs", kt) for kt in range(nkt)],
                 writes=[("vsb", vb)], dma_q="dY")

            def pv(te, kt, g=g, vb=vb, ob=ob, nkt=nkt):
                out = []
                st = kt == 0
                sp = kt == nkt - 1
                for h in range(4):
                    out.append(te.matmul(
                        o_ps[ob][:, 128 * h:128 * h + 128],
                        etb[:, 512 * kt + 128 * h:512 * kt + 128 * h + 128],
                        v_sb[vb][:, 128 * kt:128 * kt + 128],
                        start=(st and h == 0), stop=(sp and h == 3)))
                    out.append(te.matmul(
                        den_ps[:, 4 * g + h:4 * g + h + 1],
                        etb[:, 512 * kt + 128 * h:512 * kt + 128 * h + 128],
                        onesb[:, 0:1],
                        start=(st and h == 0), stop=(sp and h == 3)))
                return out

            def emit_pv(kt, pv=pv, vb=vb, ob=ob):
                p.op("T", lambda te, kt=kt, pv=pv: pv(te, kt),
                     reads=[("et", kt), ("vsb", vb), ("onesb",)],
                     writes=K_ops[ob] + K_den, waw_free=True)

            npair = nkt // 2
            for pp in range(npair):
                k0 = 2 * pp
                s2b = pp % 2
                for half in range(2):
                    p.op("T", lambda te, g=g, kt=k0 + half, s2b=s2b,
                         half=half: te.matmul(
                            s2[s2b][:, 512 * half:512 * half + 512],
                            Kd[g][:, 128 * kt:128 * kt + 128],
                            Qg[:, g, :], start=True, stop=True),
                         reads=[("Kd", k0 + half), ("Qg",)],
                         writes=[K_s2[s2b][half]])
                p.op("S", lambda s, k0=k0, s2b=s2b: s.activation(
                        etb[:, 512 * k0:512 * k0 + 1024], s2[s2b][:],
                        AF.Exp, bias=c127[:, 3:4]),
                     reads=K_s2[s2b] + [("c127",)],
                     writes=[("et", k0), ("et", k0 + 1)])
                if pp == npair - 1:
                    for half in range(2):
                        kt = k0 + half
                        p.op("V", lambda v, kt=kt, w=half, b=b: v.tensor_tensor(
                                etb[:, 512 * kt:512 * kt + 512],
                                etb[:, 512 * kt:512 * kt + 512],
                                msk[b][:, 512 * w:512 * w + 512], ALU.mult),
                             reads=[("et", kt), ("msk", b)], writes=[("et", kt)])
                if pp >= 1:
                    emit_pv(k0 - 2)
                    emit_pv(k0 - 1)
            emit_pv(nkt - 2)
            emit_pv(nkt - 1)

            p.op("V", lambda v, g=g: v.reciprocal(rden[:, 4 * g:4 * g + 4],
                                                  den_ps[:, 4 * g:4 * g + 4]),
                 reads=K_den, writes=[("rden", g)])
            p.op("V", lambda v, g=g, ob=ob, b=b: [v.scalar_tensor_tensor(
                    x1sb[:, 512 * g + 128 * h:512 * g + 128 * h + 128],
                    o_ps[ob][:, 128 * h:128 * h + 128],
                    rden[:, 4 * g + h:4 * g + h + 1],
                    xst[b][:, 512 * g + 128 * h:512 * g + 128 * h + 128],
                    ALU.mult, ALU.add) for h in range(4)],
                 reads=K_ops[ob] + [("rden", g), ("xst", b)],
                 writes=[("x1sb", g)])
        p.op("S", lambda g, j=j: g.dma_start(out=d_x1[j], in_=x1sb[:]),
             reads=[("x1sb", gg) for gg in range(G)],
             writes=[("dx1", j)], dma_q="dA")
        drain_prep(per)
    drain_prep(n_prep)

    p.barrier()
    nc.sbuf_base, nc.sbuf_top = sb_savedA
    identf = SB("sb_identf", [128, 128], F32)
    x1f = SB("x1f", [128, DT * NTOK], F32)
    xqf = SB("xqf", [128, EC * NTOK], BF16)
    t1 = SB("t1", [128, E], F32)
    i32 = SB("i32", [128, 512], I32)
    xq_tm = SB("xq_tm", [128, E], BF16)
    pgrp = SB("pgrp", [128, HT * NTOK], BF16)
    swg_t = [SB(f"swg{i}", [128, E], BF16) for i in range(2)]
    swu_t = [SB(f"swu{i}", [128, E], BF16) for i in range(2)]
    swd_t = [SB(f"swd{i}", [128, FH], BF16) for i in range(2)]
    g_sb = SB("g_sb", [128, 512], BF16)
    out_sb = [SB(f"out_sb{i}", [128, NTOK], F32) for i in range(2)]

    psum_reset()
    x1tr = PS("x1tr", [128, 512], F32)
    xqtr = PS("xqtr", [128, 2048], BF16)
    g_ps = [PS(f"g_ps{i}", [128, 512], F32) for i in range(2)]
    u_ps = [PS(f"u_ps{i}", [128, 512], F32) for i in range(2)]
    d_ps0 = PS("d_ps0", [128, 512], F32)
    K_x1tr = [KEY_B[0]]
    K_xqtr = KEY_B[1:3]
    K_gps = [[KEY_B[3]], [KEY_B[4]]]
    K_ups = [[KEY_B[5]], [KEY_B[6]]]
    d_ps = [d_ps0, x1tr]
    K_dps = [[KEY_B[7]], [KEY_B[0]]]

    p.op("Y", lambda y: y.dma_start(out=identf[:], in_=d_identf),
         writes=[("identf",)], dma_q="dY")
    x1f3 = x1f[:].rearrange("p (d t) -> p d t", t=NTOK)
    xqf3 = xqf[:].rearrange("p (c t) -> p c t", t=NTOK)

    for grp in range(NG):
        for jj in range(GT):
            j4 = GT * grp + jj
            b = jj % 2
            p.op("Y", lambda y, j4=j4, b=b: y.dma_start(out=xst[b][:],
                                                        in_=d_x1[j4]),
                 reads=[("dx1", j4)], writes=[("xst", b)], dma_q="dY")
            for r in range(4):
                p.op("T", lambda t, b=b, r=r: [t.transpose(
                        x1tr[:, 128 * c:128 * c + 128],
                        xst[b][:, 512 * r + 128 * c:512 * r + 128 * c + 128],
                        identf[:]) for c in range(4)],
                     reads=[("xst", b), ("identf",)], writes=K_x1tr)
                p.op("V", lambda v, r=r, jj=jj: v.tensor_copy(
                        x1f3[:, 4 * r:4 * r + 4, 128 * jj:128 * jj + 128],
                        x1tr[:, 0:512].rearrange("p (a q) -> p a q", q=128)),
                     reads=K_x1tr, writes=[("x1f", jj, r)])
            ln_chain_s(("xst", b), xst[b][:], 1, t1[:], ("t1",))
            ln_chain_s(("t1",), t1[:], 2, t1[:], ("t1",))
            p.op("V", lambda v: v.reduce_max(st2[:, 4:5], t1[:], axis=AX,
                                             apply_absolute_value=True),
                 reads=[("t1",)], writes=[("am",)])
            p.op("V", lambda v: v.reciprocal(st2[:, 7:8], st2[:, 4:5]),
                 reads=[("am",)], writes=[("ram",)])
            p.op("V", lambda v: v.tensor_scalar(st2[:, 5:6], st2[:, 7:8],
                                                127.0, None, ALU.mult),
                 reads=[("ram",)], writes=[("s",)])
            p.op("V", lambda v: v.tensor_scalar(st2[:, 6:7], st2[:, 4:5],
                                                float(1.0 / 127.0), None,
                                                ALU.mult),
                 reads=[("am",)], writes=[("rs",)])
            for c in range(4):
                sl = slice(512 * c, 512 * c + 512)
                p.op("V", lambda v, sl=sl: v.tensor_scalar(
                        t1[:, sl], t1[:, sl], st2[:, 5:6], MAGIC,
                        ALU.mult, ALU.add),
                     reads=[("t1",), ("s",)], writes=[("t1",)])
                p.op("V", lambda v, sl=sl: v.tensor_scalar(
                        i32[:], t1[:, sl], MAGIC, None, ALU.subtract),
                     reads=[("t1",)], writes=[("i32",)])
                p.op("V", lambda v, sl=sl: v.tensor_scalar(
                        xq_tm[:, sl], i32[:], st2[:, 6:7], None, ALU.mult),
                     reads=[("i32",), ("rs",)], writes=[("xq_tm",)])
            p.op("T", lambda t: [t.transpose(xqtr[:, 128 * c:128 * c + 128],
                                             xq_tm[:, 128 * c:128 * c + 128],
                                             identb[:]) for c in range(EC)],
                 reads=[("xq_tm",), ("identb",)], writes=K_xqtr)
            p.op("V", lambda v, jj=jj: v.tensor_copy(
                    xqf3[:, :, 128 * jj:128 * jj + 128],
                    xqtr[:, 0:128 * EC].rearrange("p (c q) -> p c q", q=128)),
                 reads=K_xqtr, writes=[("xqf", jj)])

        xq_keys = [("xqf", jj) for jj in range(GT)]
        x1_keys = [("x1f", jj, r) for jj in range(GT) for r in range(4)]
        for ht in range(HT):
            hb = ht % 2
            p.op("S", lambda s, ht=ht, hb=hb: s.dma_start(
                    out=swg_t[hb][:], in_=d_sg[ht]),
                 reads=[("dsg", ht)], writes=[("swg", hb)], dma_q="dA")
            p.op("S", lambda s, ht=ht, hb=hb: s.dma_start(
                    out=swu_t[hb][:], in_=d_su[ht]),
                 reads=[("dsu", ht)], writes=[("swu", hb)], dma_q="dA")

            def gu(te, wt, pp):
                return [te.matmul(pp[:, 0:NTOK], wt[:, 128 * c:128 * c + 128],
                                  xqf[:, NTOK * c:NTOK * c + NTOK],
                                  start=(c == 0), stop=(c == EC - 1))
                        for c in range(EC)]
            p.op("T", lambda te, hb=hb: gu(te, swg_t[hb], g_ps[hb]),
                 reads=[("swg", hb)] + xq_keys, writes=K_gps[hb])
            p.op("T", lambda te, hb=hb: gu(te, swu_t[hb], u_ps[hb]),
                 reads=[("swu", hb)] + xq_keys, writes=K_ups[hb])
            p.op("S", lambda s, hb=hb: s.activation(
                    g_sb[:, 0:NTOK], g_ps[hb][:, 0:NTOK], AF.Silu),
                 reads=K_gps[hb], writes=[("g_sb",)])
            p.op("V", lambda v, ht=ht, hb=hb: v.tensor_tensor(
                    pgrp[:, NTOK * ht:NTOK * ht + NTOK],
                    u_ps[hb][:, 0:NTOK], g_sb[:, 0:NTOK], ALU.mult),
                 reads=K_ups[hb] + [("g_sb",)], writes=[("pg", ht)])

        pg_keys = [("pg", ht) for ht in range(HT)]
        for dt in range(DT):
            db = dt % 2
            p.op("S", lambda s, dt=dt, db=db: s.dma_start(
                    out=swd_t[db][:], in_=d_sd[dt]),
                 reads=[("dsd", dt, h) for h in range(max(1, FH // ncol_d))],
                 writes=[("swd", db)], dma_q="dA")
            p.op("T", lambda te, dt=dt, db=db: [te.matmul(
                    d_ps[db][:, 0:NTOK], swd_t[db][:, 128 * hc:128 * hc + 128],
                    pgrp[:, NTOK * hc:NTOK * hc + NTOK],
                    start=(hc == 0), stop=(hc == HT - 1)) for hc in range(HT)],
                 reads=[("swd", db)] + pg_keys, writes=K_dps[db])
            p.op("V", lambda v, dt=dt, db=db: v.tensor_tensor(
                    out_sb[db][:], d_ps[db][:, 0:NTOK],
                    x1f[:, NTOK * dt:NTOK * dt + NTOK], ALU.add),
                 reads=K_dps[db] + x1_keys, writes=[("osb", db)])
            p.op("Y", lambda g, grp=grp, dt=dt, db=db: g.dma_start(
                    out=d_out[grp, dt], in_=out_sb[db][:]),
                 reads=[("osb", db)], dma_q="dY")

    p.finalize()
    return nc


def _prep_inputs(cfg, x_b, par, Wq, bq, Wk, bk, Wv, bv, Wg, Wu, Wd, masks):
    NT, EC, H, G, E = cfg.NT, cfg.EC, cfg.H, cfg.G, cfg.E
    m = {}
    m["x_full"] = np.ascontiguousarray(x_b)
    m["x_own"] = np.ascontiguousarray(x_b.reshape(NT, 128, E)[par::2])
    m["wq_in"] = np.ascontiguousarray(
        Wq.reshape(H, EC, 128, 128).transpose(1, 2, 0, 3)
          .reshape(EC, 128, H * 128))
    wkv = np.concatenate([Wk, Wv], axis=0)
    m["wkv_in"] = np.ascontiguousarray(
        wkv.reshape(2 * G, EC, 128, 128).transpose(1, 2, 0, 3)
           .reshape(EC, 128, 2 * G * 128))
    m["bqT"] = np.ascontiguousarray(bq.T)
    m["bkT"] = np.ascontiguousarray(bk.T)
    m["bv_bcast"] = np.ascontiguousarray(
        np.broadcast_to(bv.reshape(1, G * 128), (128, G * 128)))
    m["w_g"] = np.ascontiguousarray(Wg)
    m["w_u"] = np.ascontiguousarray(Wu)
    m["w_d"] = np.ascontiguousarray(Wd)
    m["mask_tab"] = masks
    m["identb"] = np.eye(128, dtype=BF16NP)
    m["identf"] = np.eye(128, dtype=np.float32)
    m["onesb"] = np.ones((128, 8), dtype=BF16NP)
    cc = np.zeros((128, 8), dtype=np.float32)
    cc[:, 0] = 127.0
    cc[:, 1] = EPS1
    cc[:, 2] = EPS2
    cc[:, 3] = -EXPC
    m["c127"] = cc
    return m


def _masks(cfg, par):
    NO = cfg.NO
    tab = np.zeros((NO, 128, 2, 4, 128), dtype=BF16NP)
    kp = np.arange(128)[:, None]
    qf = np.arange(128)[None, :]
    tri = (kp <= qf).astype(BF16NP)
    for j in range(NO):
        if par == 0:
            tab[j, :, 0] = tri[:, None, :]
            tab[j, :, 1] = 0.0
        else:
            tab[j, :, 0] = 1.0
            tab[j, :, 1] = tri[:, None, :]
    return np.ascontiguousarray(tab.reshape(NO, 128, 1024))


def kernel(L0_attn_x, L0_attn_Wq, L0_attn_bq, L0_attn_Wk, L0_attn_bk,
           L0_attn_Wv, L0_attn_bv, L0_ffn_W_g, L0_ffn_W_u, L0_ffn_W_d):
    cfg = Cfg()
    x = np.asarray(L0_attn_x, dtype=np.float32)
    B, S, E = x.shape
    nc = build_program(cfg)
    args = [np.asarray(a, np.float32) for a in
            (L0_attn_Wq, L0_attn_bq, L0_attn_Wk, L0_attn_bk,
             L0_attn_Wv, L0_attn_bv, L0_ffn_W_g, L0_ffn_W_u, L0_ffn_W_d)]
    in_maps = []
    for c in range(8):
        b, par = c // 2, c % 2
        in_maps.append(_prep_inputs(cfg, x[b], par, *args, _masks(cfg, par)))
    res = run_bass_kernel_spmd(nc, in_maps, core_ids=list(range(8)))
    out = np.empty((B, S, E), dtype=np.float32)
    NO, DT, GT, NG = cfg.NO, cfg.DT, cfg.GT, cfg.NG
    for c in range(8):
        b, par = c // 2, c % 2
        o = np.asarray(res.results[c]["out"])
        o = o.reshape(NG, DT, 128, GT, 128).transpose(0, 3, 4, 1, 2)
        o = o.reshape(NO, 128, DT * 128)
        out[b].reshape(cfg.NT, 128, E)[par::2] = o
    return out


# version 18
# speedup vs baseline: 1.0939x; 1.0923x over previous
import numpy as np
import ml_dtypes
import concourse.bass as bass
import concourse.mybir as mybir
from concourse.bass_utils import run_bass_kernel_spmd

F32 = mybir.dt.float32
BF16 = mybir.dt.bfloat16
I32 = mybir.dt.int32
AX = mybir.AxisListType.X
AF = mybir.ActivationFunctionType
ALU = mybir.AluOpType
BF16NP = ml_dtypes.bfloat16

MAGIC = float(1.5 * 2 ** 23)
EXPC = 16.0
RSQ = float(1.0 / np.sqrt(128.0))
EPS1 = 1e-5
EPS2 = 1e-8


class Cfg:
    def __init__(self, NT=32, HT=64, GT=4):
        self.NT = NT
        self.NO = NT // 2
        self.EC = 16
        self.H = 16
        self.G = 4
        self.HT = HT
        self.DT = 16
        self.GT = GT
        self.NG = self.NO // GT
        self.E = self.EC * 128
        self.FH = HT * 128


class Op:
    __slots__ = ("eng", "emit", "deps", "need_inc", "sem_val", "dma_q", "idx")

    def __init__(self, eng, emit, dma_q, idx):
        self.eng = eng
        self.emit = emit
        self.deps = []
        self.need_inc = dma_q is not None
        self.sem_val = None
        self.dma_q = dma_q
        self.idx = idx


class Prog:

    ENGS = ("Y", "S", "V", "T", "G")
    NPOOL = 8

    def __init__(self, nc):
        self.nc = nc
        self.ops = []
        self.writer = {}
        self.readers = {}
        self.last_on = {}
        self.barrier_deps = None
        self.barrier_seen = set()
        self.qidx = {"dY": 0, "dA": 0, "dG": 0}
        self.counts = {}
        self.throttle = {}

    def barrier(self):
        self.barrier_deps = [o for o in self.last_on.values()]
        self.barrier_seen = set()

    def op(self, eng, emit, reads=(), writes=(), waw_free=False, dma_q=None):
        rec = Op(eng, emit, dma_q, len(self.ops))
        lane = eng
        if dma_q is not None:
            k = self.qidx[dma_q]
            self.qidx[dma_q] = k + 1
            slot = (dma_q, k % self.NPOOL)
            v = self.counts.get(slot, 0) + 16
            self.counts[slot] = v
            rec.sem_val = (slot, v)
            if v > 16:
                self.throttle[rec.idx] = (slot, v - 16)
            lane = slot
        deps = {}

        def dep(pr):
            if pr is not None and pr is not rec:
                deps[pr.idx] = pr

        if self.barrier_deps is not None:
            if lane not in self.barrier_seen:
                self.barrier_seen.add(lane)
                for pr in self.barrier_deps:
                    dep(pr)
        for k in reads:
            dep(self.writer.get(k))
        for k in writes:
            w = self.writer.get(k)
            if w is not None and not (waw_free and w.eng == "T" and eng == "T"
                                      and w.dma_q is None and dma_q is None):
                dep(w)
            for r in self.readers.get(k, ()):
                dep(r)
        for k in reads:
            self.readers.setdefault(k, []).append(rec)
        for k in writes:
            self.writer[k] = rec
            self.readers[k] = []
        for pr in deps.values():
            pr.need_inc = True
            rec.deps.append(pr)
        self.ops.append(rec)
        self.last_on[lane] = rec
        return rec

    def finalize(self):
        nc = self.nc
        sems = {}
        for e in self.ENGS:
            sems[e] = nc.alloc_semaphore(f"sem_{e}")
        for q in ("dY", "dA", "dG"):
            for i in range(self.NPOOL):
                sems[(q, i)] = nc.alloc_semaphore(f"sem_{q}{i}")
        counts = dict.fromkeys(sems, 0)
        counts.update(self.counts)
        throttle = self.throttle
        for rec in self.ops:
            if rec.dma_q is None and rec.need_inc:
                counts[rec.eng] += 1
                rec.sem_val = (rec.eng, counts[rec.eng])
        totals = dict(counts)
        by_eng = {e: [] for e in self.ENGS}
        for rec in self.ops:
            by_eng[rec.eng].append(rec)

        with nc.Block() as block:
            def emit_engine(ename, engobj):
                water = dict.fromkeys(sems, 0)
                for rec in by_eng[ename]:
                    for pr in rec.deps:
                        s, v = pr.sem_val
                        if water[s] < v:
                            engobj.wait_ge(sems[s], v)
                            water[s] = v
                    th = throttle.get(rec.idx)
                    if th is not None and water[th[0]] < th[1]:
                        engobj.wait_ge(sems[th[0]], th[1])
                        water[th[0]] = th[1]
                    insts = rec.emit(engobj)
                    if rec.sem_val is not None:
                        s, v = rec.sem_val
                        last = insts[-1] if isinstance(insts, (list, tuple)) else insts
                        last.then_inc(sems[s], 16 if rec.dma_q else 1)
                for q, owner in (("dY", "Y"), ("dA", "S"), ("dG", "G")):
                    if owner == ename:
                        for i in range(self.NPOOL):
                            if totals[(q, i)] > 0 and water[(q, i)] < totals[(q, i)]:
                                engobj.wait_ge(sems[(q, i)], totals[(q, i)])

            for ename, secname in (("Y", "sync"), ("S", "scalar"),
                                   ("V", "vector"), ("T", "tensor"),
                                   ("G", "gpsimd")):
                getattr(block, secname)(
                    lambda eo, _n=ename: emit_engine(_n, eo))
        return totals


def build_program(cfg: Cfg):
    nc = bass.Bass("TRN2", target_bir_lowering=False)
    NT, NO, EC, H, G, HT, DT, GT, NG = (cfg.NT, cfg.NO, cfg.EC, cfg.H,
                                        cfg.G, cfg.HT, cfg.DT, cfg.GT, cfg.NG)
    E, FH = cfg.E, cfg.FH
    NTOK = GT * 128

    d_x = nc.dram_tensor("x_full", [NT * 128, E], F32, kind="ExternalInput").ap()
    d_xo = nc.dram_tensor("x_own", [NO, 128, E], F32, kind="ExternalInput").ap()
    d_wq = nc.dram_tensor("wq_in", [EC, 128, H * 128], F32, kind="ExternalInput").ap()
    d_wkv = nc.dram_tensor("wkv_in", [EC, 128, 2 * G * 128], F32, kind="ExternalInput").ap()
    d_bq = nc.dram_tensor("bqT", [128, H], F32, kind="ExternalInput").ap()
    d_bk = nc.dram_tensor("bkT", [128, G], F32, kind="ExternalInput").ap()
    d_bv = nc.dram_tensor("bv_bcast", [128, G * 128], F32, kind="ExternalInput").ap()
    d_wg = nc.dram_tensor("w_g", [FH, E], F32, kind="ExternalInput").ap()
    d_wu = nc.dram_tensor("w_u", [FH, E], F32, kind="ExternalInput").ap()
    d_wd = nc.dram_tensor("w_d", [E, FH], F32, kind="ExternalInput").ap()
    d_mask = nc.dram_tensor("mask_tab", [NO, 128, 1024], BF16, kind="ExternalInput").ap()
    d_identb = nc.dram_tensor("identb", [128, 128], BF16, kind="ExternalInput").ap()
    d_identf = nc.dram_tensor("identf", [128, 128], F32, kind="ExternalInput").ap()
    d_onesb = nc.dram_tensor("onesb", [128, 8], BF16, kind="ExternalInput").ap()
    d_c127 = nc.dram_tensor("c127", [128, 8], F32, kind="ExternalInput").ap()

    d_vs = nc.dram_tensor("v_scr", [NT, 128, G * 128], BF16).ap()
    d_qs = nc.dram_tensor("q_scr", [NO, 128, H * 128], BF16).ap()
    d_x1 = nc.dram_tensor("x1_scr", [NO, 128, E], F32).ap()
    d_sg = nc.dram_tensor("swg_scr", [HT, 128, E], BF16).ap()
    d_su = nc.dram_tensor("swu_scr", [HT, 128, E], BF16).ap()
    d_sd = nc.dram_tensor("swd_scr", [DT, 128, FH], BF16).ap()
    d_out = nc.dram_tensor("out", [NG, DT, 128, NTOK], F32, kind="ExternalOutput").ap()

    p = Prog(nc)
    SB = nc.alloc_sbuf_tensor
    PS = nc.alloc_psum_tensor

    identb = SB("sb_identb", [128, 128], BF16)
    onesb = SB("sb_onesb", [128, 8], BF16)
    c127 = SB("sb_c127", [128, 8], F32)
    bqs = SB("sb_bqs", [128, H], F32)
    bkT = SB("sb_bkT", [128, G], F32)
    xst = [SB(f"xst{i}", [128, E], F32) for i in range(2)]
    stats = SB("stats", [128, 32], F32)
    st2 = SB("st2", [128, 12], F32)

    KEY_B = [("PB", i) for i in range(8)]
    psum0 = (nc.psum_base, nc.psum_top)

    def psum_reset():
        nc.psum_base, nc.psum_top = psum0

    kvA = PS("kvA", [128, 1024], F32)
    kvB = PS("kvB", [128, 1024], F32)
    xtr1 = PS("xtr1", [128, 2048], BF16)
    ktr1 = PS("ktr1", [128, 1024], BF16)
    wtr = PS("wtr", [128, 1024], BF16)
    K_kvA, K_kvB = KEY_B[0:2], KEY_B[2:4]
    K_xtr1, K_ktr1, K_wtr = KEY_B[4:6], [KEY_B[6]], [KEY_B[7]]

    sb_savedA = (nc.sbuf_base, nc.sbuf_top)
    xln_tm = [SB(f"xln_tm{i}", [128, E], BF16) for i in range(2)]
    xln_fm = [SB(f"xln_fm{i}", [128, E], BF16) for i in range(2)]
    Kd = [SB(f"Kd{g}", [128, NT * 128], BF16) for g in range(G)]
    wst = [SB(f"wst{i}", [128, E], BF16) for i in range(2)]
    wtg = SB("wtg0", [128, E], BF16)
    Wkv_sb = SB("Wkv_sb", [128, EC, 2 * G * 128], BF16)
    ktm = SB("ktm", [128, G * 128], BF16)
    vtl = [SB(f"vtl{i}", [128, G * 128], BF16) for i in range(2)]
    bvb = SB("sb_bvb", [128, G * 128], F32)
    sb_savedB = (nc.sbuf_base, nc.sbuf_top)
    Wq_sb = SB("Wq_sb", [128, EC, H * 128], BF16)
    qtm_sb = SB("qtm_sb", [128, H * 128], BF16)
    qto_sb = SB("qto_sb", [128, H * 128], BF16)

    def ln_chain(src_key, src_ap, eps_col, out_ap, out_key):
        nchunk = E // 512
        p.op("V", lambda v: [v.bn_stats(stats[:, 6 * c:6 * c + 6],
                                        src_ap[:, 512 * c:512 * c + 512])
                             for c in range(nchunk)],
             reads=[src_key], writes=[("bn",)])
        p.op("V", lambda v: v.bn_aggr(st2[:, 0:2], stats[:, 0:6 * nchunk]),
             reads=[("bn",)], writes=[("mv",)])
        p.op("S", lambda s: s.activation(st2[:, 2:3], st2[:, 1:2], AF.Sqrt,
                                         bias=c127[:, eps_col:eps_col + 1]),
             reads=[("mv",), ("c127",)], writes=[("std",)])
        p.op("V", lambda v: v.reciprocal(st2[:, 3:4], st2[:, 2:3]),
             reads=[("std",)], writes=[("rstd",)])
        p.op("V", lambda v: v.tensor_scalar(out_ap, src_ap, st2[:, 0:1],
                                            st2[:, 3:4], ALU.subtract,
                                            ALU.mult),
             reads=[src_key, ("mv",), ("rstd",)], writes=[out_key])

    def ln_chain_s(src_key, src_ap, eps_col, out_ap, out_key):
        nchunk = E // 512
        p.op("V", lambda v: [v.bn_stats(stats[:, 6 * c:6 * c + 6],
                                        src_ap[:, 512 * c:512 * c + 512])
                             for c in range(nchunk)],
             reads=[src_key], writes=[("bn",)])
        p.op("V", lambda v: v.bn_aggr(st2[:, 0:2], stats[:, 0:6 * nchunk]),
             reads=[("bn",)], writes=[("mv",)])
        p.op("S", lambda s: s.activation(st2[:, 2:3], st2[:, 1:2], AF.Sqrt,
                                         bias=c127[:, eps_col:eps_col + 1]),
             reads=[("mv",), ("c127",)], writes=[("std",)])
        p.op("V", lambda v: v.reciprocal(st2[:, 3:4], st2[:, 2:3]),
             reads=[("std",)], writes=[("rstd",)])
        p.op("V", lambda v: v.tensor_scalar(st2[:, 8:9], st2[:, 0:1],
                                            st2[:, 3:4], -1.0,
                                            ALU.mult, ALU.mult),
             reads=[("mv",), ("rstd",)], writes=[("negmr",)])
        p.op("S", lambda s: s.activation(out_ap, src_ap, AF.Identity,
                                         bias=st2[:, 8:9], scale=st2[:, 3:4]),
             reads=[src_key, ("negmr",), ("rstd",)], writes=[out_key])

    def transpose16(src_ap, src_key, ps, ps_keys, dst_ap, dst_key):
        p.op("T", lambda t: [t.transpose(ps[:, 128 * c:128 * c + 128],
                                         src_ap[:, 128 * c:128 * c + 128],
                                         identb[:]) for c in range(EC)],
             reads=[src_key, ("identb",)], writes=ps_keys)
        p.op("V", lambda v: v.tensor_copy(dst_ap, ps[:, 0:128 * EC]),
             reads=ps_keys, writes=[dst_key])

    for sb_t, key, src in ((identb, "identb", d_identb),
                           (onesb, "onesb", d_onesb), (c127, "c127", d_c127),
                           (bqs, "bqs", d_bq), (bkT, "bkT", d_bk),
                           (bvb, "bvb", d_bv)):
        p.op("Y", lambda y, sb_t=sb_t, src=src: y.dma_start(out=sb_t[:], in_=src),
             writes=[(key,)], dma_q="dY")
    p.op("S", lambda s: s.mul(bqs[:], bqs[:], RSQ),
         reads=[("bqs",)], writes=[("bqs",)])
    for ech in range(EC):
        p.op("G", lambda g, c=ech: g.dma_start(out=Wkv_sb[:, c, :], in_=d_wkv[c]),
             writes=[("wkv", ech)], dma_q="dG")
        p.op("G", lambda g, c=ech: g.dma_start(out=Wq_sb[:, c, :], in_=d_wq[c]),
             writes=[("wq", ech)], dma_q="dG")

    prep_srcs = []

    def prep_issue(k):
        src_ap, scr_ap, ncol, scr_key = prep_srcs[k]
        wb = k % 2
        p.op("G", lambda g, wb=wb, ncol=ncol, src_ap=src_ap: g.dma_start(
                out=wst[wb][:, 0:ncol], in_=src_ap),
             writes=[("wst", wb)], dma_q="dG")

    def prep_compute(k):
        src_ap, scr_ap, ncol, scr_key = prep_srcs[k]
        wb = k % 2
        nb = ncol // 128
        for h0 in range(0, nb, 8):
            hn = min(8, nb - h0)
            p.op("T", lambda t, h0=h0, hn=hn, wb=wb: [t.transpose(
                    wtr[:, 128 * c:128 * c + 128],
                    wst[wb][:, 128 * (h0 + c):128 * (h0 + c) + 128],
                    identb[:]) for c in range(hn)],
                 reads=[("wst", wb), ("identb",)], writes=K_wtr)
            p.op("S", lambda s, h0=h0, hn=hn: s.activation(
                    wtg[:, 128 * h0:128 * (h0 + hn)], wtr[:, 0:128 * hn],
                    AF.Sign),
                 reads=K_wtr, writes=[("wtg",)])
        p.op("G", lambda g, ncol=ncol, scr_ap=scr_ap: g.dma_start(
                out=scr_ap, in_=wtg[:, 0:ncol]),
             reads=[("wtg",)], writes=[scr_key], dma_q="dG")

    for ht in range(HT):
        prep_srcs.append((d_wg[128 * ht:128 * ht + 128, :], d_sg[ht], E,
                          ("dsg", ht)))
        prep_srcs.append((d_wu[128 * ht:128 * ht + 128, :], d_su[ht], E,
                          ("dsu", ht)))
    ncol_d = min(E, FH)
    for dt in range(DT):
        for hcg in range(max(1, FH // ncol_d)):
            prep_srcs.append((
                d_wd[128 * dt:128 * dt + 128, ncol_d * hcg:ncol_d * (hcg + 1)],
                d_sd[dt][:, ncol_d * hcg:ncol_d * (hcg + 1)], ncol_d,
                ("dsd", dt, hcg)))
    n_prep = len(prep_srcs)
    state = {"issued": 0, "done": 0}

    def drain_prep(k):
        while k > 0 and state["done"] < n_prep:
            while state["issued"] < min(n_prep, state["done"] + 2):
                prep_issue(state["issued"])
                state["issued"] += 1
            prep_compute(state["done"])
            state["done"] += 1
            k -= 1

    per1 = max(1, (n_prep // 3) // max(1, NO))
    per2 = -(-n_prep // max(1, NO))

    x_t = d_x.rearrange("(n p) e -> n p e", p=128)
    psum_reset()
    q2_ps = PS("q2_ps", [128, 2048], F32)
    qtr_ps = PS("qtr_ps", [128, 2048], BF16)
    K_q2 = KEY_B[0:4]
    K_qtr = KEY_B[4:6]

    def ph1_head(j):
        b = j % 2
        p.op("Y", lambda y, j=j, b=b: y.dma_start(out=xst[b][:], in_=d_xo[j]),
             writes=[("xst", b)], dma_q="dY")
        ln_chain(("xst", b), xst[b][:], 1, xln_tm[b][:], ("xln_tm", b))

    ph1_head(0)
    for j in range(NO):
        b = j % 2
        transpose16(xln_tm[b][:], ("xln_tm", b), qtr_ps, K_qtr, xln_fm[b][:],
                    ("xln_fm", b))
        if j + 1 < NO:
            ph1_head(j + 1)

        def q_mms(te, b=b):
            out = []
            for c in range(EC):
                for q4 in range(4):
                    out.append(te.matmul(
                        q2_ps[:, 512 * q4:512 * q4 + 512],
                        xln_fm[b][:, 128 * c:128 * c + 128],
                        Wq_sb[:, c, 512 * q4:512 * q4 + 512],
                        start=(c == 0), stop=(c == EC - 1)))
            return out
        p.op("T", q_mms, reads=[("xln_fm", b)] + [("wq", c) for c in range(EC)],
             writes=K_q2)
        p.op("S", lambda s: s.activation(qtm_sb[:], q2_ps[:], AF.Identity,
                                         scale=RSQ),
             reads=K_q2, writes=[("qtm",)])
        p.op("T", lambda te: [te.transpose(qtr_ps[:, 128 * h:128 * h + 128],
                                           qtm_sb[:, 128 * h:128 * h + 128],
                                           identb[:]) for h in range(H)],
             reads=[("qtm",), ("identb",)], writes=K_qtr)
        p.op("S", lambda s: [s.activation(
                qto_sb[:, 128 * h:128 * h + 128],
                qtr_ps[:, 128 * h:128 * h + 128],
                AF.Identity, bias=bqs[:, h:h + 1]) for h in range(H)],
             reads=K_qtr + [("bqs",)], writes=[("qto",)])
        p.op("S", lambda g, j=j: g.dma_start(out=d_qs[j], in_=qto_sb[:]),
             reads=[("qto",)], writes=[("dqs", j)], dma_q="dA")
        drain_prep(per1)

    p.barrier()
    nc.sbuf_base, nc.sbuf_top = sb_savedB
    etb = SB("etb", [128, 8 * 512], BF16)
    v_sb = [SB(f"v_sb{i}", [128, NT * 128], BF16) for i in range(2)]
    Qg = [SB(f"Qg{i}", [128, H * 128], BF16) for i in range(2)]
    x1sb = SB("x1sb", [128, E], F32)
    msk = [SB(f"msk{i}", [128, 1024], BF16) for i in range(2)]
    rden = SB("rden", [128, 16], F32)
    xkv = [SB(f"xkv{i}", [128, E], F32) for i in range(2)]

    psum_reset()
    s2 = [PS(f"s2_{i}", [128, 1024], F32) for i in range(2)]
    o_ps = [PS(f"o_ps{i}", [128, 512], F32) for i in range(2)]
    den_ps = PS("den_ps", [128, 512], F32)
    K_s2 = [KEY_B[0:2], KEY_B[2:4]]
    K_ops = [[KEY_B[4]], [KEY_B[5]]]
    K_den = [KEY_B[6]]
    psum_reset()
    _d6 = PS("_d6", [128, 512 * 6], F32)
    ktr2 = PS("ktr2", [128, 1024], BF16)
    K_ktr2 = K_den

    def ph2_head(j):
        b = j % 2
        p.op("Y", lambda y, j=j, b=b: y.dma_start(out=xst[b][:], in_=d_xo[j]),
             writes=[("xst", b)], dma_q="dY")
        p.op("Y", lambda y, j=j, b=b: y.dma_start(out=msk[b][:], in_=d_mask[j]),
             writes=[("msk", b)], dma_q="dY")
        p.op("Y", lambda y, j=j, b=b: y.dma_start(out=Qg[b][:], in_=d_qs[j]),
             reads=[("dqs", j)], writes=[("Qg", b)], dma_q="dY")
        for u in range(2):
            t = 2 * j + u
            p.op("Y", lambda y, t=t, u=u: y.dma_start(out=xkv[u][:], in_=x_t[t]),
                 writes=[("xkv", u)], dma_q="dY")
            ln_chain(("xkv", u), xkv[u][:], 1, xln_tm[u][:], ("xln_tm", u))

    ph2_head(0)
    for j in range(NO):
        b = j % 2
        nkt = 2 * j + 2
        for u in range(2):
            t = 2 * j + u
            transpose16(xln_tm[u][:], ("xln_tm", u), qtr_ps, K_qtr,
                        xln_fm[u][:], ("xln_fm", u))

            def kv_mms(te, u=u):
                out = []
                for c in range(EC):
                    for half in range(2):
                        out.append(te.matmul(
                            s2[u][:, 512 * half:512 * half + 512],
                            xln_fm[u][:, 128 * c:128 * c + 128],
                            Wkv_sb[:, c, 512 * half:512 * half + 512],
                            start=(c == 0), stop=(c == EC - 1)))
                return out
            p.op("T", kv_mms, reads=[("xln_fm", u)] + [("wkv", c) for c in range(EC)],
                 writes=K_s2[u])
            p.op("S", lambda s, u=u: s.activation(ktm[:], s2[u][:, 0:512], AF.Copy),
                 reads=K_s2[u], writes=[("ktm",)])
            p.op("T", lambda te: [te.transpose(ktr2[:, 128 * g:128 * g + 128],
                                               ktm[:, 128 * g:128 * g + 128],
                                               identb[:]) for g in range(G)],
                 reads=[("ktm",), ("identb",)], writes=K_ktr2)
            p.op("S", lambda s, t=t: [s.activation(
                    Kd[g][:, 128 * t:128 * t + 128],
                    ktr2[:, 128 * g:128 * g + 128],
                    AF.Identity, bias=bkT[:, g:g + 1]) for g in range(G)],
                 reads=K_ktr2 + [("bkT",)], writes=[("Kd", t)])
            p.op("V", lambda v, u=u: v.tensor_tensor(vtl[u][:], s2[u][:, 512:1024],
                                                     bvb[:], ALU.add),
                 reads=K_s2[u] + [("bvb",)], writes=[("vtl", u)])
            p.op("S", lambda g, t=t, u=u: g.dma_start(out=d_vs[t], in_=vtl[u][:]),
                 reads=[("vtl", u)], writes=[("dvs", t)], dma_q="dA")
        if j + 1 < NO:
            ph2_head(j + 1)

        for g in range(G):
            vb = g % 2
            ob = g % 2
            if j > 0:
                p.op("Y", lambda y, g=g, vb=vb, nkt=nkt: y.dma_start(
                        out=v_sb[vb][:, 0:128 * (nkt - 2)].rearrange(
                            "p (n d) -> p n d", d=128),
                        in_=d_vs[0:nkt - 2, :, 128 * g:128 * g + 128]
                            .transpose([1, 0, 2])),
                     reads=[("dvs", kt) for kt in range(nkt - 2)],
                     writes=[("vsb", vb)], dma_q="dY")

            def pv(te, kt, g=g, vb=vb, ob=ob, nkt=nkt):
                out = []
                st = kt == 0
                sp = kt == nkt - 1
                kr = kt % 8
                if kt >= nkt - 2:
                    vap = vtl[kt - (nkt - 2)][:, 128 * g:128 * g + 128]
                else:
                    vap = v_sb[vb][:, 128 * kt:128 * kt + 128]
                for h in range(4):
                    out.append(te.matmul(
                        o_ps[ob][:, 128 * h:128 * h + 128],
                        etb[:, 512 * kr + 128 * h:512 * kr + 128 * h + 128],
                        vap,
                        start=(st and h == 0), stop=(sp and h == 3)))
                    out.append(te.matmul(
                        den_ps[:, 4 * g + h:4 * g + h + 1],
                        etb[:, 512 * kr + 128 * h:512 * kr + 128 * h + 128],
                        onesb[:, 0:1],
                        start=(st and h == 0), stop=(sp and h == 3)))
                return out

            def emit_pv(kt, pv=pv, vb=vb, ob=ob, nkt=nkt):
                vkey = ("vtl", kt - (nkt - 2)) if kt >= nkt - 2 else ("vsb", vb)
                p.op("T", lambda te, kt=kt, pv=pv: pv(te, kt),
                     reads=[("et", kt % 8), vkey, ("onesb",)],
                     writes=K_ops[ob] + K_den, waw_free=True)

            npair = nkt // 2
            for pp in range(npair):
                k0 = 2 * pp
                s2b = pp % 2
                for half in range(2):
                    p.op("T", lambda te, g=g, kt=k0 + half, s2b=s2b,
                         half=half, b=b: te.matmul(
                            s2[s2b][:, 512 * half:512 * half + 512],
                            Kd[g][:, 128 * kt:128 * kt + 128],
                            Qg[b][:, 512 * g:512 * g + 512],
                            start=True, stop=True),
                         reads=[("Kd", k0 + half), ("Qg", b)],
                         writes=[K_s2[s2b][half]])
                p.op("S", lambda s, k0=k0, s2b=s2b: s.activation(
                        etb[:, 512 * (k0 % 8):512 * (k0 % 8) + 1024], s2[s2b][:],
                        AF.Exp, bias=c127[:, 3:4]),
                     reads=K_s2[s2b] + [("c127",)],
                     writes=[("et", k0 % 8), ("et", (k0 + 1) % 8)])
                if pp == npair - 1:
                    for half in range(2):
                        kt = k0 + half
                        p.op("V", lambda v, kt=kt, w=half, b=b: v.tensor_tensor(
                                etb[:, 512 * (kt % 8):512 * (kt % 8) + 512],
                                etb[:, 512 * (kt % 8):512 * (kt % 8) + 512],
                                msk[b][:, 512 * w:512 * w + 512], ALU.mult),
                             reads=[("et", kt % 8), ("msk", b)],
                             writes=[("et", kt % 8)])
                if pp >= 1:
                    emit_pv(k0 - 2)
                    emit_pv(k0 - 1)
            emit_pv(nkt - 2)
            emit_pv(nkt - 1)

            p.op("V", lambda v, g=g: v.reciprocal(rden[:, 4 * g:4 * g + 4],
                                                  den_ps[:, 4 * g:4 * g + 4]),
                 reads=K_den, writes=[("rden", g)])
            p.op("V", lambda v, g=g, ob=ob, b=b: [v.scalar_tensor_tensor(
                    x1sb[:, 512 * g + 128 * h:512 * g + 128 * h + 128],
                    o_ps[ob][:, 128 * h:128 * h + 128],
                    rden[:, 4 * g + h:4 * g + h + 1],
                    xst[b][:, 512 * g + 128 * h:512 * g + 128 * h + 128],
                    ALU.mult, ALU.add) for h in range(4)],
                 reads=K_ops[ob] + [("rden", g), ("xst", b)],
                 writes=[("x1sb", g)])
        p.op("S", lambda g, j=j: g.dma_start(out=d_x1[j], in_=x1sb[:]),
             reads=[("x1sb", gg) for gg in range(G)],
             writes=[("dx1", j)], dma_q="dA")
        drain_prep(per2)
    drain_prep(n_prep)

    p.barrier()
    nc.sbuf_base, nc.sbuf_top = sb_savedA
    identf = SB("sb_identf", [128, 128], F32)
    x1f = SB("x1f", [128, DT * NTOK], F32)
    xqf = SB("xqf", [128, EC * NTOK], BF16)
    t1 = SB("t1", [128, E], F32)
    i32 = SB("i32", [128, 512], I32)
    xq_tm = SB("xq_tm", [128, E], BF16)
    pgrp = SB("pgrp", [128, HT * NTOK], BF16)
    swg_t = [SB(f"swg{i}", [128, E], BF16) for i in range(2)]
    swu_t = [SB(f"swu{i}", [128, E], BF16) for i in range(2)]
    swd_t = [SB(f"swd{i}", [128, FH], BF16) for i in range(2)]
    g_sb = SB("g_sb", [128, 512], BF16)
    out_sb = [SB(f"out_sb{i}", [128, NTOK], F32) for i in range(2)]

    psum_reset()
    x1tr = PS("x1tr", [128, 512], F32)
    xqtr = PS("xqtr", [128, 2048], BF16)
    g_ps = [PS(f"g_ps{i}", [128, 512], F32) for i in range(2)]
    u_ps = [PS(f"u_ps{i}", [128, 512], F32) for i in range(2)]
    d_ps0 = PS("d_ps0", [128, 512], F32)
    K_x1tr = [KEY_B[0]]
    K_xqtr = KEY_B[1:3]
    K_gps = [[KEY_B[3]], [KEY_B[4]]]
    K_ups = [[KEY_B[5]], [KEY_B[6]]]
    d_ps = [d_ps0, x1tr]
    K_dps = [[KEY_B[7]], [KEY_B[0]]]

    p.op("Y", lambda y: y.dma_start(out=identf[:], in_=d_identf),
         writes=[("identf",)], dma_q="dY")
    x1f3 = x1f[:].rearrange("p (d t) -> p d t", t=NTOK)
    xqf3 = xqf[:].rearrange("p (c t) -> p c t", t=NTOK)

    for grp in range(NG):
        for jj in range(GT):
            j4 = GT * grp + jj
            b = jj % 2
            p.op("Y", lambda y, j4=j4, b=b: y.dma_start(out=xst[b][:],
                                                        in_=d_x1[j4]),
                 reads=[("dx1", j4)], writes=[("xst", b)], dma_q="dY")
            for r in range(4):
                p.op("T", lambda t, b=b, r=r: [t.transpose(
                        x1tr[:, 128 * c:128 * c + 128],
                        xst[b][:, 512 * r + 128 * c:512 * r + 128 * c + 128],
                        identf[:]) for c in range(4)],
                     reads=[("xst", b), ("identf",)], writes=K_x1tr)
                p.op("V", lambda v, r=r, jj=jj: v.tensor_copy(
                        x1f3[:, 4 * r:4 * r + 4, 128 * jj:128 * jj + 128],
                        x1tr[:, 0:512].rearrange("p (a q) -> p a q", q=128)),
                     reads=K_x1tr, writes=[("x1f", jj, r)])
            ln_chain_s(("xst", b), xst[b][:], 1, t1[:], ("t1",))
            ln_chain_s(("t1",), t1[:], 2, t1[:], ("t1",))
            p.op("V", lambda v: v.reduce_max(st2[:, 4:5], t1[:], axis=AX,
                                             apply_absolute_value=True),
                 reads=[("t1",)], writes=[("am",)])
            p.op("V", lambda v: v.reciprocal(st2[:, 7:8], st2[:, 4:5]),
                 reads=[("am",)], writes=[("ram",)])
            p.op("V", lambda v: v.tensor_scalar(st2[:, 5:6], st2[:, 7:8],
                                                127.0, None, ALU.mult),
                 reads=[("ram",)], writes=[("s",)])
            p.op("V", lambda v: v.tensor_scalar(st2[:, 6:7], st2[:, 4:5],
                                                float(1.0 / 127.0), None,
                                                ALU.mult),
                 reads=[("am",)], writes=[("rs",)])
            for c in range(4):
                sl = slice(512 * c, 512 * c + 512)
                p.op("V", lambda v, sl=sl: v.tensor_scalar(
                        t1[:, sl], t1[:, sl], st2[:, 5:6], MAGIC,
                        ALU.mult, ALU.add),
                     reads=[("t1",), ("s",)], writes=[("t1",)])
                p.op("V", lambda v, sl=sl: v.tensor_scalar(
                        i32[:], t1[:, sl], MAGIC, None, ALU.subtract),
                     reads=[("t1",)], writes=[("i32",)])
                p.op("V", lambda v, sl=sl: v.tensor_scalar(
                        xq_tm[:, sl], i32[:], st2[:, 6:7], None, ALU.mult),
                     reads=[("i32",), ("rs",)], writes=[("xq_tm",)])
            p.op("T", lambda t: [t.transpose(xqtr[:, 128 * c:128 * c + 128],
                                             xq_tm[:, 128 * c:128 * c + 128],
                                             identb[:]) for c in range(EC)],
                 reads=[("xq_tm",), ("identb",)], writes=K_xqtr)
            p.op("V", lambda v, jj=jj: v.tensor_copy(
                    xqf3[:, :, 128 * jj:128 * jj + 128],
                    xqtr[:, 0:128 * EC].rearrange("p (c q) -> p c q", q=128)),
                 reads=K_xqtr, writes=[("xqf", jj)])

        xq_keys = [("xqf", jj) for jj in range(GT)]
        x1_keys = [("x1f", jj, r) for jj in range(GT) for r in range(4)]
        for ht in range(HT):
            hb = ht % 2
            p.op("S", lambda s, ht=ht, hb=hb: s.dma_start(
                    out=swg_t[hb][:], in_=d_sg[ht]),
                 reads=[("dsg", ht)], writes=[("swg", hb)], dma_q="dA")
            p.op("S", lambda s, ht=ht, hb=hb: s.dma_start(
                    out=swu_t[hb][:], in_=d_su[ht]),
                 reads=[("dsu", ht)], writes=[("swu", hb)], dma_q="dA")

            def gu(te, wt, pp):
                return [te.matmul(pp[:, 0:NTOK], wt[:, 128 * c:128 * c + 128],
                                  xqf[:, NTOK * c:NTOK * c + NTOK],
                                  start=(c == 0), stop=(c == EC - 1))
                        for c in range(EC)]
            p.op("T", lambda te, hb=hb: gu(te, swg_t[hb], g_ps[hb]),
                 reads=[("swg", hb)] + xq_keys, writes=K_gps[hb])
            p.op("T", lambda te, hb=hb: gu(te, swu_t[hb], u_ps[hb]),
                 reads=[("swu", hb)] + xq_keys, writes=K_ups[hb])
            p.op("S", lambda s, hb=hb: s.activation(
                    g_sb[:, 0:NTOK], g_ps[hb][:, 0:NTOK], AF.Silu),
                 reads=K_gps[hb], writes=[("g_sb",)])
            p.op("V", lambda v, ht=ht, hb=hb: v.tensor_tensor(
                    pgrp[:, NTOK * ht:NTOK * ht + NTOK],
                    u_ps[hb][:, 0:NTOK], g_sb[:, 0:NTOK], ALU.mult),
                 reads=K_ups[hb] + [("g_sb",)], writes=[("pg", ht)])

        pg_keys = [("pg", ht) for ht in range(HT)]
        for dt in range(DT):
            db = dt % 2
            p.op("S", lambda s, dt=dt, db=db: s.dma_start(
                    out=swd_t[db][:], in_=d_sd[dt]),
                 reads=[("dsd", dt, h) for h in range(max(1, FH // ncol_d))],
                 writes=[("swd", db)], dma_q="dA")
            p.op("T", lambda te, dt=dt, db=db: [te.matmul(
                    d_ps[db][:, 0:NTOK], swd_t[db][:, 128 * hc:128 * hc + 128],
                    pgrp[:, NTOK * hc:NTOK * hc + NTOK],
                    start=(hc == 0), stop=(hc == HT - 1)) for hc in range(HT)],
                 reads=[("swd", db)] + pg_keys, writes=K_dps[db])
            p.op("V", lambda v, dt=dt, db=db: v.tensor_tensor(
                    out_sb[db][:], d_ps[db][:, 0:NTOK],
                    x1f[:, NTOK * dt:NTOK * dt + NTOK], ALU.add),
                 reads=K_dps[db] + x1_keys, writes=[("osb", db)])
            p.op("Y", lambda g, grp=grp, dt=dt, db=db: g.dma_start(
                    out=d_out[grp, dt], in_=out_sb[db][:]),
                 reads=[("osb", db)], dma_q="dY")

    p.finalize()
    return nc


def _prep_inputs(cfg, x_b, par, Wq, bq, Wk, bk, Wv, bv, Wg, Wu, Wd, masks):
    NT, EC, H, G, E = cfg.NT, cfg.EC, cfg.H, cfg.G, cfg.E
    m = {}
    m["x_full"] = np.ascontiguousarray(x_b)
    m["x_own"] = np.ascontiguousarray(x_b.reshape(NT, 128, E)[par::2])
    m["wq_in"] = np.ascontiguousarray(
        Wq.reshape(H, EC, 128, 128).transpose(1, 2, 0, 3)
          .reshape(EC, 128, H * 128))
    wkv = np.concatenate([Wk, Wv], axis=0)
    m["wkv_in"] = np.ascontiguousarray(
        wkv.reshape(2 * G, EC, 128, 128).transpose(1, 2, 0, 3)
           .reshape(EC, 128, 2 * G * 128))
    m["bqT"] = np.ascontiguousarray(bq.T)
    m["bkT"] = np.ascontiguousarray(bk.T)
    m["bv_bcast"] = np.ascontiguousarray(
        np.broadcast_to(bv.reshape(1, G * 128), (128, G * 128)))
    m["w_g"] = np.ascontiguousarray(Wg)
    m["w_u"] = np.ascontiguousarray(Wu)
    m["w_d"] = np.ascontiguousarray(Wd)
    m["mask_tab"] = masks
    m["identb"] = np.eye(128, dtype=BF16NP)
    m["identf"] = np.eye(128, dtype=np.float32)
    m["onesb"] = np.ones((128, 8), dtype=BF16NP)
    cc = np.zeros((128, 8), dtype=np.float32)
    cc[:, 0] = 127.0
    cc[:, 1] = EPS1
    cc[:, 2] = EPS2
    cc[:, 3] = -EXPC
    m["c127"] = cc
    return m


def _masks(cfg, par):
    NO = cfg.NO
    tab = np.zeros((NO, 128, 2, 4, 128), dtype=BF16NP)
    kp = np.arange(128)[:, None]
    qf = np.arange(128)[None, :]
    tri = (kp <= qf).astype(BF16NP)
    for j in range(NO):
        if par == 0:
            tab[j, :, 0] = tri[:, None, :]
            tab[j, :, 1] = 0.0
        else:
            tab[j, :, 0] = 1.0
            tab[j, :, 1] = tri[:, None, :]
    return np.ascontiguousarray(tab.reshape(NO, 128, 1024))


def kernel(L0_attn_x, L0_attn_Wq, L0_attn_bq, L0_attn_Wk, L0_attn_bk,
           L0_attn_Wv, L0_attn_bv, L0_ffn_W_g, L0_ffn_W_u, L0_ffn_W_d):
    cfg = Cfg()
    x = np.asarray(L0_attn_x, dtype=np.float32)
    B, S, E = x.shape
    nc = build_program(cfg)
    args = [np.asarray(a, np.float32) for a in
            (L0_attn_Wq, L0_attn_bq, L0_attn_Wk, L0_attn_bk,
             L0_attn_Wv, L0_attn_bv, L0_ffn_W_g, L0_ffn_W_u, L0_ffn_W_d)]
    in_maps = []
    for c in range(8):
        b, par = c // 2, c % 2
        in_maps.append(_prep_inputs(cfg, x[b], par, *args, _masks(cfg, par)))
    res = run_bass_kernel_spmd(nc, in_maps, core_ids=list(range(8)))
    out = np.empty((B, S, E), dtype=np.float32)
    NO, DT, GT, NG = cfg.NO, cfg.DT, cfg.GT, cfg.NG
    for c in range(8):
        b, par = c // 2, c % 2
        o = np.asarray(res.results[c]["out"])
        o = o.reshape(NG, DT, 128, GT, 128).transpose(0, 3, 4, 1, 2)
        o = o.reshape(NO, 128, DT * 128)
        out[b].reshape(cfg.NT, 128, E)[par::2] = o
    return out


# version 19
# speedup vs baseline: 1.1341x; 1.0368x over previous
import numpy as np
import ml_dtypes
import concourse.bass as bass
import concourse.mybir as mybir
from concourse.bass_utils import run_bass_kernel_spmd

F32 = mybir.dt.float32
BF16 = mybir.dt.bfloat16
I32 = mybir.dt.int32
AX = mybir.AxisListType.X
AF = mybir.ActivationFunctionType
ALU = mybir.AluOpType
BF16NP = ml_dtypes.bfloat16

MAGIC = float(1.5 * 2 ** 23)
EXPC = 16.0
RSQ = float(1.0 / np.sqrt(128.0))
EPS1 = 1e-5
EPS2 = 1e-8


class Cfg:
    def __init__(self, NT=32, HT=64, GT=4):
        self.NT = NT
        self.NO = NT // 2
        self.EC = 16
        self.H = 16
        self.G = 4
        self.HT = HT
        self.DT = 16
        self.GT = GT
        self.NG = self.NO // GT
        self.E = self.EC * 128
        self.FH = HT * 128


class Op:
    __slots__ = ("eng", "emit", "deps", "need_inc", "sem_val", "dma_q", "idx")

    def __init__(self, eng, emit, dma_q, idx):
        self.eng = eng
        self.emit = emit
        self.deps = []
        self.need_inc = dma_q is not None
        self.sem_val = None
        self.dma_q = dma_q
        self.idx = idx


class Prog:

    ENGS = ("Y", "S", "V", "T", "G")
    NPOOL = 8

    def __init__(self, nc):
        self.nc = nc
        self.ops = []
        self.writer = {}
        self.readers = {}
        self.last_on = {}
        self.barrier_deps = None
        self.barrier_seen = set()
        self.qidx = {"dY": 0, "dA": 0, "dG": 0}
        self.counts = {}
        self.throttle = {}

    def barrier(self):
        self.barrier_deps = [o for o in self.last_on.values()]
        self.barrier_seen = set()

    def op(self, eng, emit, reads=(), writes=(), waw_free=False, dma_q=None):
        rec = Op(eng, emit, dma_q, len(self.ops))
        lane = eng
        if dma_q is not None:
            k = self.qidx[dma_q]
            self.qidx[dma_q] = k + 1
            slot = (dma_q, k % self.NPOOL)
            v = self.counts.get(slot, 0) + 16
            self.counts[slot] = v
            rec.sem_val = (slot, v)
            if v > 16:
                self.throttle[rec.idx] = (slot, v - 16)
            lane = slot
        deps = {}

        def dep(pr):
            if pr is not None and pr is not rec:
                deps[pr.idx] = pr

        if self.barrier_deps is not None:
            if lane not in self.barrier_seen:
                self.barrier_seen.add(lane)
                for pr in self.barrier_deps:
                    dep(pr)
        for k in reads:
            dep(self.writer.get(k))
        for k in writes:
            w = self.writer.get(k)
            if w is not None and not (waw_free and w.eng == "T" and eng == "T"
                                      and w.dma_q is None and dma_q is None):
                dep(w)
            for r in self.readers.get(k, ()):
                dep(r)
        for k in reads:
            self.readers.setdefault(k, []).append(rec)
        for k in writes:
            self.writer[k] = rec
            self.readers[k] = []
        for pr in deps.values():
            pr.need_inc = True
            rec.deps.append(pr)
        self.ops.append(rec)
        self.last_on[lane] = rec
        return rec

    def finalize(self):
        nc = self.nc
        sems = {}
        for e in self.ENGS:
            sems[e] = nc.alloc_semaphore(f"sem_{e}")
        for q in ("dY", "dA", "dG"):
            for i in range(self.NPOOL):
                sems[(q, i)] = nc.alloc_semaphore(f"sem_{q}{i}")
        counts = dict.fromkeys(sems, 0)
        counts.update(self.counts)
        throttle = self.throttle
        for rec in self.ops:
            if rec.dma_q is None and rec.need_inc:
                counts[rec.eng] += 1
                rec.sem_val = (rec.eng, counts[rec.eng])
        totals = dict(counts)
        by_eng = {e: [] for e in self.ENGS}
        for rec in self.ops:
            by_eng[rec.eng].append(rec)

        with nc.Block() as block:
            def emit_engine(ename, engobj):
                water = dict.fromkeys(sems, 0)
                for rec in by_eng[ename]:
                    for pr in rec.deps:
                        s, v = pr.sem_val
                        if water[s] < v:
                            engobj.wait_ge(sems[s], v)
                            water[s] = v
                    th = throttle.get(rec.idx)
                    if th is not None and water[th[0]] < th[1]:
                        engobj.wait_ge(sems[th[0]], th[1])
                        water[th[0]] = th[1]
                    insts = rec.emit(engobj)
                    if rec.sem_val is not None:
                        s, v = rec.sem_val
                        last = insts[-1] if isinstance(insts, (list, tuple)) else insts
                        last.then_inc(sems[s], 16 if rec.dma_q else 1)
                for q, owner in (("dY", "Y"), ("dA", "S"), ("dG", "G")):
                    if owner == ename:
                        for i in range(self.NPOOL):
                            if totals[(q, i)] > 0 and water[(q, i)] < totals[(q, i)]:
                                engobj.wait_ge(sems[(q, i)], totals[(q, i)])

            for ename, secname in (("Y", "sync"), ("S", "scalar"),
                                   ("V", "vector"), ("T", "tensor"),
                                   ("G", "gpsimd")):
                getattr(block, secname)(
                    lambda eo, _n=ename: emit_engine(_n, eo))
        return totals


def build_program(cfg: Cfg):
    nc = bass.Bass("TRN2", target_bir_lowering=False)
    NT, NO, EC, H, G, HT, DT, GT, NG = (cfg.NT, cfg.NO, cfg.EC, cfg.H,
                                        cfg.G, cfg.HT, cfg.DT, cfg.GT, cfg.NG)
    E, FH = cfg.E, cfg.FH
    NTOK = GT * 128

    d_x = nc.dram_tensor("x_full", [NT * 128, E], F32, kind="ExternalInput").ap()
    d_xo = nc.dram_tensor("x_own", [NO, 128, E], F32, kind="ExternalInput").ap()
    d_wq = nc.dram_tensor("wq_in", [EC, 128, H * 128], F32, kind="ExternalInput").ap()
    d_wkv = nc.dram_tensor("wkv_in", [EC, 128, 2 * G * 128], F32, kind="ExternalInput").ap()
    d_bq = nc.dram_tensor("bqT", [128, H], F32, kind="ExternalInput").ap()
    d_bk = nc.dram_tensor("bkT", [128, G], F32, kind="ExternalInput").ap()
    d_bv = nc.dram_tensor("bv_bcast", [128, G * 128], F32, kind="ExternalInput").ap()
    d_wg = nc.dram_tensor("w_g", [FH, E], F32, kind="ExternalInput").ap()
    d_wu = nc.dram_tensor("w_u", [FH, E], F32, kind="ExternalInput").ap()
    d_wd = nc.dram_tensor("w_d", [E, FH], F32, kind="ExternalInput").ap()
    d_mask = nc.dram_tensor("mask_tab", [NO, 128, 1024], BF16, kind="ExternalInput").ap()
    d_identb = nc.dram_tensor("identb", [128, 128], BF16, kind="ExternalInput").ap()
    d_identf = nc.dram_tensor("identf", [128, 128], F32, kind="ExternalInput").ap()
    d_onesb = nc.dram_tensor("onesb", [128, 8], BF16, kind="ExternalInput").ap()
    d_c127 = nc.dram_tensor("c127", [128, 8], F32, kind="ExternalInput").ap()

    d_vs = nc.dram_tensor("v_scr", [NT, 128, G * 128], BF16).ap()
    d_x1 = nc.dram_tensor("x1_scr", [NO, 128, E], F32).ap()
    d_sg = nc.dram_tensor("swg_scr", [HT, 128, E], BF16).ap()
    d_su = nc.dram_tensor("swu_scr", [HT, 128, E], BF16).ap()
    d_sd = nc.dram_tensor("swd_scr", [DT, 128, FH], BF16).ap()
    d_out = nc.dram_tensor("out", [NG, DT, 128, NTOK], F32, kind="ExternalOutput").ap()

    p = Prog(nc)
    SB = nc.alloc_sbuf_tensor
    PS = nc.alloc_psum_tensor

    identb = SB("sb_identb", [128, 128], BF16)
    onesb = SB("sb_onesb", [128, 8], BF16)
    c127 = SB("sb_c127", [128, 8], F32)
    bqs = SB("sb_bqs", [128, H], F32)
    bkT = SB("sb_bkT", [128, G], F32)
    xst = [SB(f"xst{i}", [128, E], F32) for i in range(2)]
    stats = SB("stats", [128, 32], F32)
    st2 = SB("st2", [128, 12], F32)

    KEY_B = [("PB", i) for i in range(8)]
    psum0 = (nc.psum_base, nc.psum_top)

    def psum_reset():
        nc.psum_base, nc.psum_top = psum0

    kvA = PS("kvA", [128, 1024], F32)
    kvB = PS("kvB", [128, 1024], F32)
    xtr1 = PS("xtr1", [128, 2048], BF16)
    ktr1 = PS("ktr1", [128, 1024], BF16)
    wtr = PS("wtr", [128, 1024], BF16)
    K_kvA, K_kvB = KEY_B[0:2], KEY_B[2:4]
    K_xtr1, K_ktr1, K_wtr = KEY_B[4:6], [KEY_B[6]], [KEY_B[7]]

    sb_savedA = (nc.sbuf_base, nc.sbuf_top)
    xln_tm = [SB(f"xln_tm{i}", [128, E], BF16) for i in range(2)]
    xln_fm = [SB(f"xln_fm{i}", [128, E], BF16) for i in range(2)]
    Kd = [SB(f"Kd{g}", [128, NT * 128], BF16) for g in range(G)]
    wst = [SB(f"wst{i}", [128, E], BF16) for i in range(2)]
    wtg = SB("wtg0", [128, E], BF16)
    sb_savedB = (nc.sbuf_base, nc.sbuf_top)
    Wkv_sb = SB("Wkv_sb", [128, EC, 2 * G * 128], BF16)
    ktm = SB("ktm", [128, G * 128], BF16)
    vtl = [SB(f"vtl{i}", [128, G * 128], BF16) for i in range(2)]
    bvb = SB("sb_bvb", [128, G * 128], F32)

    def ln_chain(src_key, src_ap, eps_col, out_ap, out_key):
        nchunk = E // 512
        p.op("V", lambda v: [v.bn_stats(stats[:, 6 * c:6 * c + 6],
                                        src_ap[:, 512 * c:512 * c + 512])
                             for c in range(nchunk)],
             reads=[src_key], writes=[("bn",)])
        p.op("V", lambda v: v.bn_aggr(st2[:, 0:2], stats[:, 0:6 * nchunk]),
             reads=[("bn",)], writes=[("mv",)])
        p.op("S", lambda s: s.activation(st2[:, 2:3], st2[:, 1:2], AF.Sqrt,
                                         bias=c127[:, eps_col:eps_col + 1]),
             reads=[("mv",), ("c127",)], writes=[("std",)])
        p.op("V", lambda v: v.reciprocal(st2[:, 3:4], st2[:, 2:3]),
             reads=[("std",)], writes=[("rstd",)])
        p.op("V", lambda v: v.tensor_scalar(out_ap, src_ap, st2[:, 0:1],
                                            st2[:, 3:4], ALU.subtract,
                                            ALU.mult),
             reads=[src_key, ("mv",), ("rstd",)], writes=[out_key])

    def ln_chain_s(src_key, src_ap, eps_col, out_ap, out_key):
        nchunk = E // 512
        p.op("V", lambda v: [v.bn_stats(stats[:, 6 * c:6 * c + 6],
                                        src_ap[:, 512 * c:512 * c + 512])
                             for c in range(nchunk)],
             reads=[src_key], writes=[("bn",)])
        p.op("V", lambda v: v.bn_aggr(st2[:, 0:2], stats[:, 0:6 * nchunk]),
             reads=[("bn",)], writes=[("mv",)])
        p.op("S", lambda s: s.activation(st2[:, 2:3], st2[:, 1:2], AF.Sqrt,
                                         bias=c127[:, eps_col:eps_col + 1]),
             reads=[("mv",), ("c127",)], writes=[("std",)])
        p.op("V", lambda v: v.reciprocal(st2[:, 3:4], st2[:, 2:3]),
             reads=[("std",)], writes=[("rstd",)])
        p.op("V", lambda v: v.tensor_scalar(st2[:, 8:9], st2[:, 0:1],
                                            st2[:, 3:4], -1.0,
                                            ALU.mult, ALU.mult),
             reads=[("mv",), ("rstd",)], writes=[("negmr",)])
        p.op("S", lambda s: s.activation(out_ap, src_ap, AF.Identity,
                                         bias=st2[:, 8:9], scale=st2[:, 3:4]),
             reads=[src_key, ("negmr",), ("rstd",)], writes=[out_key])

    def transpose16(src_ap, src_key, ps, ps_keys, dst_ap, dst_key):
        p.op("T", lambda t: [t.transpose(ps[:, 128 * c:128 * c + 128],
                                         src_ap[:, 128 * c:128 * c + 128],
                                         identb[:]) for c in range(EC)],
             reads=[src_key, ("identb",)], writes=ps_keys)
        p.op("V", lambda v: v.tensor_copy(dst_ap, ps[:, 0:128 * EC]),
             reads=ps_keys, writes=[dst_key])

    for sb_t, key, src in ((identb, "identb", d_identb),
                           (onesb, "onesb", d_onesb), (c127, "c127", d_c127),
                           (bqs, "bqs", d_bq), (bkT, "bkT", d_bk),
                           (bvb, "bvb", d_bv)):
        p.op("Y", lambda y, sb_t=sb_t, src=src: y.dma_start(out=sb_t[:], in_=src),
             writes=[(key,)], dma_q="dY")
    p.op("S", lambda s: s.mul(bqs[:], bqs[:], RSQ),
         reads=[("bqs",)], writes=[("bqs",)])
    for ech in range(EC):
        p.op("G", lambda g, c=ech: g.dma_start(out=Wkv_sb[:, c, :], in_=d_wkv[c]),
             writes=[("wkv", ech)], dma_q="dG")

    prep_srcs = []

    def prep_issue(k):
        src_ap, scr_ap, ncol, scr_key = prep_srcs[k]
        wb = k % 2
        p.op("G", lambda g, wb=wb, ncol=ncol, src_ap=src_ap: g.dma_start(
                out=wst[wb][:, 0:ncol], in_=src_ap),
             writes=[("wst", wb)], dma_q="dG")

    def prep_compute(k):
        src_ap, scr_ap, ncol, scr_key = prep_srcs[k]
        wb = k % 2
        nb = ncol // 128
        for h0 in range(0, nb, 8):
            hn = min(8, nb - h0)
            p.op("T", lambda t, h0=h0, hn=hn, wb=wb: [t.transpose(
                    wtr[:, 128 * c:128 * c + 128],
                    wst[wb][:, 128 * (h0 + c):128 * (h0 + c) + 128],
                    identb[:]) for c in range(hn)],
                 reads=[("wst", wb), ("identb",)], writes=K_wtr)
            p.op("S", lambda s, h0=h0, hn=hn: s.activation(
                    wtg[:, 128 * h0:128 * (h0 + hn)], wtr[:, 0:128 * hn],
                    AF.Sign),
                 reads=K_wtr, writes=[("wtg",)])
        p.op("G", lambda g, ncol=ncol, scr_ap=scr_ap: g.dma_start(
                out=scr_ap, in_=wtg[:, 0:ncol]),
             reads=[("wtg",)], writes=[scr_key], dma_q="dG")

    for ht in range(HT):
        prep_srcs.append((d_wg[128 * ht:128 * ht + 128, :], d_sg[ht], E,
                          ("dsg", ht)))
        prep_srcs.append((d_wu[128 * ht:128 * ht + 128, :], d_su[ht], E,
                          ("dsu", ht)))
    ncol_d = min(E, FH)
    for dt in range(DT):
        for hcg in range(max(1, FH // ncol_d)):
            prep_srcs.append((
                d_wd[128 * dt:128 * dt + 128, ncol_d * hcg:ncol_d * (hcg + 1)],
                d_sd[dt][:, ncol_d * hcg:ncol_d * (hcg + 1)], ncol_d,
                ("dsd", dt, hcg)))
    n_prep = len(prep_srcs)
    state = {"issued": 0, "done": 0}

    def drain_prep(k):
        while k > 0 and state["done"] < n_prep:
            while state["issued"] < min(n_prep, state["done"] + 2):
                prep_issue(state["issued"])
                state["issued"] += 1
            prep_compute(state["done"])
            state["done"] += 1
            k -= 1

    per = -(-n_prep // (NT + NO - 2)) if (NT + NO) > 2 else n_prep

    x_t = d_x.rearrange("(n p) e -> n p e", p=128)

    def ph1_head(t):
        b = t % 2
        p.op("Y", lambda y, t=t, b=b: y.dma_start(out=xst[b][:], in_=x_t[t]),
             writes=[("xst", b)], dma_q="dY")
        ln_chain(("xst", b), xst[b][:], 1, xln_tm[b][:], ("xln_tm", b))

    ph1_head(0)
    for t in range(NT):
        b = t % 2
        transpose16(xln_tm[b][:], ("xln_tm", b), xtr1, K_xtr1, xln_fm[b][:],
                    ("xln_fm", b))
        if t + 1 < NT:
            ph1_head(t + 1)
        kv = kvA if b == 0 else kvB
        K_kv = K_kvA if b == 0 else K_kvB

        def kv_mms(te, kv=kv, b=b):
            out = []
            for c in range(EC):
                for half in range(2):
                    out.append(te.matmul(
                        kv[:, 512 * half:512 * half + 512],
                        xln_fm[b][:, 128 * c:128 * c + 128],
                        Wkv_sb[:, c, 512 * half:512 * half + 512],
                        start=(c == 0), stop=(c == EC - 1)))
            return out
        p.op("T", kv_mms, reads=[("xln_fm", b)] + [("wkv", c) for c in range(EC)],
             writes=K_kv)
        p.op("S", lambda s, kv=kv: s.activation(ktm[:], kv[:, 0:512], AF.Copy),
             reads=K_kv, writes=[("ktm",)])
        p.op("T", lambda te: [te.transpose(ktr1[:, 128 * g:128 * g + 128],
                                           ktm[:, 128 * g:128 * g + 128],
                                           identb[:]) for g in range(G)],
             reads=[("ktm",), ("identb",)], writes=K_ktr1)
        p.op("S", lambda s, t=t: [s.activation(
                Kd[g][:, 128 * t:128 * t + 128],
                ktr1[:, 128 * g:128 * g + 128],
                AF.Identity, bias=bkT[:, g:g + 1]) for g in range(G)],
             reads=K_ktr1 + [("bkT",)], writes=[("Kd", t)])
        p.op("V", lambda v, kv=kv, b=b: v.tensor_tensor(vtl[b][:], kv[:, 512:1024],
                                                   bvb[:], ALU.add),
             reads=K_kv + [("bvb",)], writes=[("vtl", b)])
        p.op("S", lambda g, t=t, b=b: g.dma_start(out=d_vs[t], in_=vtl[b][:]),
             reads=[("vtl", b)], writes=[("dvs", t)], dma_q="dA")
        if t >= 1:
            drain_prep(per)

    nc.sbuf_base, nc.sbuf_top = sb_savedB
    Wq_sb = SB("Wq_sb", [128, EC, H * 128], BF16)
    etb = SB("etb", [128, NT * 512], BF16)
    v_sb = [SB(f"v_sb{i}", [128, NT * 128], BF16) for i in range(2)]
    Qg = SB("Qg", [128, G, 512], BF16)
    x1sb = SB("x1sb", [128, E], F32)
    msk = [SB(f"msk{i}", [128, 1024], BF16) for i in range(2)]
    rden = SB("rden", [128, 16], F32)

    psum_reset()
    s2 = [PS(f"s2_{i}", [128, 1024], F32) for i in range(2)]
    o_ps = [PS(f"o_ps{i}", [128, 512], F32) for i in range(2)]
    den_ps = PS("den_ps", [128, 512], F32)
    K_s2 = [KEY_B[0:2], KEY_B[2:4]]
    K_ops = [[KEY_B[4]], [KEY_B[5]]]
    K_den = [KEY_B[6]]
    psum_reset()
    xtr2 = PS("xtr2", [128, 2048], BF16)
    K_xtr2 = KEY_B[0:2]
    psum_reset()
    _dummy2 = PS("dummy2", [128, 1024], F32)
    q_ps = [PS(f"q_ps{g}", [128, 512], F32) for g in range(G)]
    K_qps = [[KEY_B[2 + g]] for g in range(G)]

    for ech in range(EC):
        extra = [("ktm",), ("vtl", 0), ("vtl", 1), ("bvb",)] if ech == 0 else []
        p.op("G", lambda g, c=ech: g.dma_start(out=Wq_sb[:, c, :], in_=d_wq[c]),
             writes=[("wq", ech), ("wkv", ech)] + extra, dma_q="dG")

    def ph2_head(j):
        b = j % 2
        p.op("Y", lambda y, j=j, b=b: y.dma_start(out=xst[b][:], in_=d_xo[j]),
             writes=[("xst", b)], dma_q="dY")
        p.op("Y", lambda y, j=j, b=b: y.dma_start(out=msk[b][:], in_=d_mask[j]),
             writes=[("msk", b)], dma_q="dY")
        ln_chain(("xst", b), xst[b][:], 1, xln_tm[b][:], ("xln_tm", b))

    ph2_head(0)
    for j in range(NO):
        b = j % 2
        nkt = 2 * j + 2
        transpose16(xln_tm[b][:], ("xln_tm", b), xtr2, K_xtr2, xln_fm[b][:],
                    ("xln_fm", b))

        def q_mms(te, b=b):
            out = []
            for c in range(EC):
                for h in range(H):
                    out.append(te.matmul(
                        q_ps[h // 4][:, 128 * (h % 4):128 * (h % 4) + 128],
                        Wq_sb[:, c, 128 * h:128 * h + 128],
                        xln_fm[b][:, 128 * c:128 * c + 128],
                        start=(c == 0 and h % 4 == 0),
                        stop=(c == EC - 1 and h % 4 == 3)))
            return out
        p.op("T", q_mms, reads=[("xln_fm", b)] + [("wq", c) for c in range(EC)],
             writes=sum(K_qps, []))
        p.op("S", lambda s: [s.activation(
                Qg[:, h // 4, 128 * (h % 4):128 * (h % 4) + 128],
                q_ps[h // 4][:, 128 * (h % 4):128 * (h % 4) + 128],
                AF.Identity, bias=bqs[:, h:h + 1], scale=RSQ) for h in range(H)],
             reads=sum(K_qps, []) + [("bqs",)], writes=[("Qg",)])
        if j + 1 < NO:
            ph2_head(j + 1)

        for g in range(G):
            vb = g % 2
            ob = g % 2
            p.op("Y", lambda y, g=g, vb=vb, nkt=nkt: y.dma_start(
                    out=v_sb[vb][:, 0:128 * nkt].rearrange(
                        "p (n d) -> p n d", d=128),
                    in_=d_vs[0:nkt, :, 128 * g:128 * g + 128].transpose([1, 0, 2])),
                 reads=[("dvs", kt) for kt in range(nkt)],
                 writes=[("vsb", vb)], dma_q="dY")

            def pv(te, kt, g=g, vb=vb, ob=ob, nkt=nkt):
                out = []
                st = kt == 0
                sp = kt == nkt - 1
                for h in range(4):
                    out.append(te.matmul(
                        o_ps[ob][:, 128 * h:128 * h + 128],
                        etb[:, 512 * kt + 128 * h:512 * kt + 128 * h + 128],
                        v_sb[vb][:, 128 * kt:128 * kt + 128],
                        start=(st and h == 0), stop=(sp and h == 3)))
                    out.append(te.matmul(
                        den_ps[:, 4 * g + h:4 * g + h + 1],
                        etb[:, 512 * kt + 128 * h:512 * kt + 128 * h + 128],
                        onesb[:, 0:1],
                        start=(st and h == 0), stop=(sp and h == 3)))
                return out

            def emit_pv(kt, pv=pv, vb=vb, ob=ob):
                p.op("T", lambda te, kt=kt, pv=pv: pv(te, kt),
                     reads=[("et", kt), ("vsb", vb), ("onesb",)],
                     writes=K_ops[ob] + K_den, waw_free=True)

            npair = nkt // 2
            for pp in range(npair):
                k0 = 2 * pp
                s2b = pp % 2
                for half in range(2):
                    p.op("T", lambda te, g=g, kt=k0 + half, s2b=s2b,
                         half=half: te.matmul(
                            s2[s2b][:, 512 * half:512 * half + 512],
                            Kd[g][:, 128 * kt:128 * kt + 128],
                            Qg[:, g, :], start=True, stop=True),
                         reads=[("Kd", k0 + half), ("Qg",)],
                         writes=[K_s2[s2b][half]])
                p.op("S", lambda s, k0=k0, s2b=s2b: s.activation(
                        etb[:, 512 * k0:512 * k0 + 1024], s2[s2b][:],
                        AF.Exp, bias=c127[:, 3:4]),
                     reads=K_s2[s2b] + [("c127",)],
                     writes=[("et", k0), ("et", k0 + 1)])
                if pp == npair - 1:
                    for half in range(2):
                        kt = k0 + half
                        p.op("V", lambda v, kt=kt, w=half, b=b: v.tensor_tensor(
                                etb[:, 512 * kt:512 * kt + 512],
                                etb[:, 512 * kt:512 * kt + 512],
                                msk[b][:, 512 * w:512 * w + 512], ALU.mult),
                             reads=[("et", kt), ("msk", b)], writes=[("et", kt)])
                if pp >= 1:
                    emit_pv(k0 - 2)
                    emit_pv(k0 - 1)
            emit_pv(nkt - 2)
            emit_pv(nkt - 1)

            p.op("V", lambda v, g=g: v.reciprocal(rden[:, 4 * g:4 * g + 4],
                                                  den_ps[:, 4 * g:4 * g + 4]),
                 reads=K_den, writes=[("rden", g)])
            p.op("V", lambda v, g=g, ob=ob, b=b: [v.scalar_tensor_tensor(
                    x1sb[:, 512 * g + 128 * h:512 * g + 128 * h + 128],
                    o_ps[ob][:, 128 * h:128 * h + 128],
                    rden[:, 4 * g + h:4 * g + h + 1],
                    xst[b][:, 512 * g + 128 * h:512 * g + 128 * h + 128],
                    ALU.mult, ALU.add) for h in range(4)],
                 reads=K_ops[ob] + [("rden", g), ("xst", b)],
                 writes=[("x1sb", g)])
        p.op("S", lambda g, j=j: g.dma_start(out=d_x1[j], in_=x1sb[:]),
             reads=[("x1sb", gg) for gg in range(G)],
             writes=[("dx1", j)], dma_q="dA")
        drain_prep(per)
    drain_prep(n_prep)

    p.barrier()
    nc.sbuf_base, nc.sbuf_top = sb_savedA
    identf = SB("sb_identf", [128, 128], F32)
    x1f = SB("x1f", [128, DT * NTOK], F32)
    xqf = SB("xqf", [128, EC * NTOK], BF16)
    t1 = SB("t1", [128, E], F32)
    i32 = SB("i32", [128, 512], I32)
    xq_tm = SB("xq_tm", [128, E], BF16)
    pgrp = SB("pgrp", [128, HT * NTOK], BF16)
    swg_t = [SB(f"swg{i}", [128, E], BF16) for i in range(2)]
    swu_t = [SB(f"swu{i}", [128, E], BF16) for i in range(2)]
    swd_t = [SB(f"swd{i}", [128, FH], BF16) for i in range(2)]
    g_sb = SB("g_sb", [128, 512], BF16)
    out_sb = [SB(f"out_sb{i}", [128, NTOK], F32) for i in range(2)]

    psum_reset()
    x1tr = PS("x1tr", [128, 512], F32)
    xqtr = PS("xqtr", [128, 2048], BF16)
    g_ps = [PS(f"g_ps{i}", [128, 512], F32) for i in range(2)]
    u_ps = [PS(f"u_ps{i}", [128, 512], F32) for i in range(2)]
    d_ps0 = PS("d_ps0", [128, 512], F32)
    K_x1tr = [KEY_B[0]]
    K_xqtr = KEY_B[1:3]
    K_gps = [[KEY_B[3]], [KEY_B[4]]]
    K_ups = [[KEY_B[5]], [KEY_B[6]]]
    d_ps = [d_ps0, x1tr]
    K_dps = [[KEY_B[7]], [KEY_B[0]]]

    p.op("Y", lambda y: y.dma_start(out=identf[:], in_=d_identf),
         writes=[("identf",)], dma_q="dY")
    x1f3 = x1f[:].rearrange("p (d t) -> p d t", t=NTOK)
    xqf3 = xqf[:].rearrange("p (c t) -> p c t", t=NTOK)

    for grp in range(NG):
        for jj in range(GT):
            j4 = GT * grp + jj
            b = jj % 2
            p.op("Y", lambda y, j4=j4, b=b: y.dma_start(out=xst[b][:],
                                                        in_=d_x1[j4]),
                 reads=[("dx1", j4)], writes=[("xst", b)], dma_q="dY")
            for r in range(4):
                p.op("T", lambda t, b=b, r=r: [t.transpose(
                        x1tr[:, 128 * c:128 * c + 128],
                        xst[b][:, 512 * r + 128 * c:512 * r + 128 * c + 128],
                        identf[:]) for c in range(4)],
                     reads=[("xst", b), ("identf",)], writes=K_x1tr)
                p.op("V", lambda v, r=r, jj=jj: v.tensor_copy(
                        x1f3[:, 4 * r:4 * r + 4, 128 * jj:128 * jj + 128],
                        x1tr[:, 0:512].rearrange("p (a q) -> p a q", q=128)),
                     reads=K_x1tr, writes=[("x1f", jj, r)])
            ln_chain_s(("xst", b), xst[b][:], 1, t1[:], ("t1",))
            ln_chain_s(("t1",), t1[:], 2, t1[:], ("t1",))
            p.op("V", lambda v: v.reduce_max(st2[:, 4:5], t1[:], axis=AX,
                                             apply_absolute_value=True),
                 reads=[("t1",)], writes=[("am",)])
            p.op("V", lambda v: v.reciprocal(st2[:, 7:8], st2[:, 4:5]),
                 reads=[("am",)], writes=[("ram",)])
            p.op("V", lambda v: v.tensor_scalar(st2[:, 5:6], st2[:, 7:8],
                                                127.0, None, ALU.mult),
                 reads=[("ram",)], writes=[("s",)])
            p.op("V", lambda v: v.tensor_scalar(st2[:, 6:7], st2[:, 4:5],
                                                float(1.0 / 127.0), None,
                                                ALU.mult),
                 reads=[("am",)], writes=[("rs",)])
            for c in range(4):
                sl = slice(512 * c, 512 * c + 512)
                p.op("S", lambda s, sl=sl: s.activation(
                        t1[:, sl], t1[:, sl], AF.Identity,
                        scale=st2[:, 5:6], bias=c127[:, 4:5]),
                     reads=[("t1",), ("s",), ("c127",)], writes=[("t1",)])
                p.op("S", lambda s, sl=sl: s.activation(
                        i32[:], t1[:, sl], AF.Identity, bias=c127[:, 5:6]),
                     reads=[("t1",), ("c127",)], writes=[("i32",)])
                p.op("S", lambda s, sl=sl: s.activation(
                        xq_tm[:, sl], i32[:], AF.Identity,
                        scale=st2[:, 6:7]),
                     reads=[("i32",), ("rs",)], writes=[("xq_tm",)])
            p.op("T", lambda t: [t.transpose(xqtr[:, 128 * c:128 * c + 128],
                                             xq_tm[:, 128 * c:128 * c + 128],
                                             identb[:]) for c in range(EC)],
                 reads=[("xq_tm",), ("identb",)], writes=K_xqtr)
            p.op("V", lambda v, jj=jj: v.tensor_copy(
                    xqf3[:, :, 128 * jj:128 * jj + 128],
                    xqtr[:, 0:128 * EC].rearrange("p (c q) -> p c q", q=128)),
                 reads=K_xqtr, writes=[("xqf", jj)])

        xq_keys = [("xqf", jj) for jj in range(GT)]
        x1_keys = [("x1f", jj, r) for jj in range(GT) for r in range(4)]
        for ht in range(HT):
            hb = ht % 2
            p.op("S", lambda s, ht=ht, hb=hb: s.dma_start(
                    out=swg_t[hb][:], in_=d_sg[ht]),
                 reads=[("dsg", ht)], writes=[("swg", hb)], dma_q="dA")
            p.op("S", lambda s, ht=ht, hb=hb: s.dma_start(
                    out=swu_t[hb][:], in_=d_su[ht]),
                 reads=[("dsu", ht)], writes=[("swu", hb)], dma_q="dA")

            def gu(te, wt, pp):
                return [te.matmul(pp[:, 0:NTOK], wt[:, 128 * c:128 * c + 128],
                                  xqf[:, NTOK * c:NTOK * c + NTOK],
                                  start=(c == 0), stop=(c == EC - 1))
                        for c in range(EC)]
            p.op("T", lambda te, hb=hb: gu(te, swg_t[hb], g_ps[hb]),
                 reads=[("swg", hb)] + xq_keys, writes=K_gps[hb])
            p.op("T", lambda te, hb=hb: gu(te, swu_t[hb], u_ps[hb]),
                 reads=[("swu", hb)] + xq_keys, writes=K_ups[hb])
            p.op("S", lambda s, hb=hb: s.activation(
                    g_sb[:, 0:NTOK], g_ps[hb][:, 0:NTOK], AF.Silu),
                 reads=K_gps[hb], writes=[("g_sb",)])
            p.op("V", lambda v, ht=ht, hb=hb: v.tensor_tensor(
                    pgrp[:, NTOK * ht:NTOK * ht + NTOK],
                    u_ps[hb][:, 0:NTOK], g_sb[:, 0:NTOK], ALU.mult),
                 reads=K_ups[hb] + [("g_sb",)], writes=[("pg", ht)])

        pg_keys = [("pg", ht) for ht in range(HT)]
        for dt in range(DT):
            db = dt % 2
            p.op("S", lambda s, dt=dt, db=db: s.dma_start(
                    out=swd_t[db][:], in_=d_sd[dt]),
                 reads=[("dsd", dt, h) for h in range(max(1, FH // ncol_d))],
                 writes=[("swd", db)], dma_q="dA")
            p.op("T", lambda te, dt=dt, db=db: [te.matmul(
                    d_ps[db][:, 0:NTOK], swd_t[db][:, 128 * hc:128 * hc + 128],
                    pgrp[:, NTOK * hc:NTOK * hc + NTOK],
                    start=(hc == 0), stop=(hc == HT - 1)) for hc in range(HT)],
                 reads=[("swd", db)] + pg_keys, writes=K_dps[db])
            p.op("V", lambda v, dt=dt, db=db: v.tensor_tensor(
                    out_sb[db][:], d_ps[db][:, 0:NTOK],
                    x1f[:, NTOK * dt:NTOK * dt + NTOK], ALU.add),
                 reads=K_dps[db] + x1_keys, writes=[("osb", db)])
            p.op("Y", lambda g, grp=grp, dt=dt, db=db: g.dma_start(
                    out=d_out[grp, dt], in_=out_sb[db][:]),
                 reads=[("osb", db)], dma_q="dY")

    p.finalize()
    return nc


def _prep_inputs(cfg, x_b, par, Wq, bq, Wk, bk, Wv, bv, Wg, Wu, Wd, masks):
    NT, EC, H, G, E = cfg.NT, cfg.EC, cfg.H, cfg.G, cfg.E
    m = {}
    m["x_full"] = np.ascontiguousarray(x_b)
    m["x_own"] = np.ascontiguousarray(x_b.reshape(NT, 128, E)[par::2])
    m["wq_in"] = np.ascontiguousarray(
        Wq.reshape(H, EC, 128, 128).transpose(1, 2, 0, 3)
          .reshape(EC, 128, H * 128))
    wkv = np.concatenate([Wk, Wv], axis=0)
    m["wkv_in"] = np.ascontiguousarray(
        wkv.reshape(2 * G, EC, 128, 128).transpose(1, 2, 0, 3)
           .reshape(EC, 128, 2 * G * 128))
    m["bqT"] = np.ascontiguousarray(bq.T)
    m["bkT"] = np.ascontiguousarray(bk.T)
    m["bv_bcast"] = np.ascontiguousarray(
        np.broadcast_to(bv.reshape(1, G * 128), (128, G * 128)))
    m["w_g"] = np.ascontiguousarray(Wg)
    m["w_u"] = np.ascontiguousarray(Wu)
    m["w_d"] = np.ascontiguousarray(Wd)
    m["mask_tab"] = masks
    m["identb"] = np.eye(128, dtype=BF16NP)
    m["identf"] = np.eye(128, dtype=np.float32)
    m["onesb"] = np.ones((128, 8), dtype=BF16NP)
    cc = np.zeros((128, 8), dtype=np.float32)
    cc[:, 0] = 127.0
    cc[:, 1] = EPS1
    cc[:, 2] = EPS2
    cc[:, 3] = -EXPC
    cc[:, 4] = MAGIC
    cc[:, 5] = -MAGIC
    m["c127"] = cc
    return m


def _masks(cfg, par):
    NO = cfg.NO
    tab = np.zeros((NO, 128, 2, 4, 128), dtype=BF16NP)
    kp = np.arange(128)[:, None]
    qf = np.arange(128)[None, :]
    tri = (kp <= qf).astype(BF16NP)
    for j in range(NO):
        if par == 0:
            tab[j, :, 0] = tri[:, None, :]
            tab[j, :, 1] = 0.0
        else:
            tab[j, :, 0] = 1.0
            tab[j, :, 1] = tri[:, None, :]
    return np.ascontiguousarray(tab.reshape(NO, 128, 1024))


def kernel(L0_attn_x, L0_attn_Wq, L0_attn_bq, L0_attn_Wk, L0_attn_bk,
           L0_attn_Wv, L0_attn_bv, L0_ffn_W_g, L0_ffn_W_u, L0_ffn_W_d):
    cfg = Cfg()
    x = np.asarray(L0_attn_x, dtype=np.float32)
    B, S, E = x.shape
    nc = build_program(cfg)
    args = [np.asarray(a, np.float32) for a in
            (L0_attn_Wq, L0_attn_bq, L0_attn_Wk, L0_attn_bk,
             L0_attn_Wv, L0_attn_bv, L0_ffn_W_g, L0_ffn_W_u, L0_ffn_W_d)]
    in_maps = []
    for c in range(8):
        b, par = c // 2, c % 2
        in_maps.append(_prep_inputs(cfg, x[b], par, *args, _masks(cfg, par)))
    res = run_bass_kernel_spmd(nc, in_maps, core_ids=list(range(8)))
    out = np.empty((B, S, E), dtype=np.float32)
    NO, DT, GT, NG = cfg.NO, cfg.DT, cfg.GT, cfg.NG
    for c in range(8):
        b, par = c // 2, c % 2
        o = np.asarray(res.results[c]["out"])
        o = o.reshape(NG, DT, 128, GT, 128).transpose(0, 3, 4, 1, 2)
        o = o.reshape(NO, 128, DT * 128)
        out[b].reshape(cfg.NT, 128, E)[par::2] = o
    return out


# version 20
# speedup vs baseline: 1.1676x; 1.0295x over previous
import numpy as np
import ml_dtypes
import concourse.bass as bass
import concourse.mybir as mybir
from concourse.bass_utils import run_bass_kernel_spmd

F32 = mybir.dt.float32
BF16 = mybir.dt.bfloat16
I32 = mybir.dt.int32
AX = mybir.AxisListType.X
AF = mybir.ActivationFunctionType
ALU = mybir.AluOpType
BF16NP = ml_dtypes.bfloat16

MAGIC = float(1.5 * 2 ** 23)
EXPC = 16.0
RSQ = float(1.0 / np.sqrt(128.0))
EPS1 = 1e-5
EPS2 = 1e-8


class Cfg:
    def __init__(self, NT=32, HT=64, GT=4):
        self.NT = NT
        self.NO = NT // 2
        self.EC = 16
        self.H = 16
        self.G = 4
        self.HT = HT
        self.DT = 16
        self.GT = GT
        self.NG = self.NO // GT
        self.E = self.EC * 128
        self.FH = HT * 128


class Op:
    __slots__ = ("eng", "emit", "deps", "need_inc", "sem_val", "dma_q", "idx")

    def __init__(self, eng, emit, dma_q, idx):
        self.eng = eng
        self.emit = emit
        self.deps = []
        self.need_inc = dma_q is not None
        self.sem_val = None
        self.dma_q = dma_q
        self.idx = idx


class Prog:

    ENGS = ("Y", "S", "V", "T", "G")
    NPOOL = 8

    def __init__(self, nc):
        self.nc = nc
        self.ops = []
        self.writer = {}
        self.readers = {}
        self.last_on = {}
        self.barrier_deps = None
        self.barrier_seen = set()
        self.qidx = {"dY": 0, "dA": 0, "dG": 0}
        self.counts = {}
        self.throttle = {}

    def barrier(self):
        self.barrier_deps = [o for o in self.last_on.values()]
        self.barrier_seen = set()

    def op(self, eng, emit, reads=(), writes=(), waw_free=False, dma_q=None):
        rec = Op(eng, emit, dma_q, len(self.ops))
        lane = eng
        if dma_q is not None:
            k = self.qidx[dma_q]
            self.qidx[dma_q] = k + 1
            slot = (dma_q, k % self.NPOOL)
            v = self.counts.get(slot, 0) + 16
            self.counts[slot] = v
            rec.sem_val = (slot, v)
            if v > 16:
                self.throttle[rec.idx] = (slot, v - 16)
            lane = slot
        deps = {}

        def dep(pr):
            if pr is not None and pr is not rec:
                deps[pr.idx] = pr

        if self.barrier_deps is not None:
            if lane not in self.barrier_seen:
                self.barrier_seen.add(lane)
                for pr in self.barrier_deps:
                    dep(pr)
        for k in reads:
            dep(self.writer.get(k))
        for k in writes:
            w = self.writer.get(k)
            if w is not None and not (waw_free and w.eng == "T" and eng == "T"
                                      and w.dma_q is None and dma_q is None):
                dep(w)
            for r in self.readers.get(k, ()):
                dep(r)
        for k in reads:
            self.readers.setdefault(k, []).append(rec)
        for k in writes:
            self.writer[k] = rec
            self.readers[k] = []
        for pr in deps.values():
            pr.need_inc = True
            rec.deps.append(pr)
        self.ops.append(rec)
        self.last_on[lane] = rec
        return rec

    def finalize(self):
        nc = self.nc
        sems = {}
        for e in self.ENGS:
            sems[e] = nc.alloc_semaphore(f"sem_{e}")
        for q in ("dY", "dA", "dG"):
            for i in range(self.NPOOL):
                sems[(q, i)] = nc.alloc_semaphore(f"sem_{q}{i}")
        counts = dict.fromkeys(sems, 0)
        counts.update(self.counts)
        throttle = self.throttle
        for rec in self.ops:
            if rec.dma_q is None and rec.need_inc:
                counts[rec.eng] += 1
                rec.sem_val = (rec.eng, counts[rec.eng])
        totals = dict(counts)
        by_eng = {e: [] for e in self.ENGS}
        for rec in self.ops:
            by_eng[rec.eng].append(rec)

        with nc.Block() as block:
            def emit_engine(ename, engobj):
                water = dict.fromkeys(sems, 0)
                for rec in by_eng[ename]:
                    for pr in rec.deps:
                        s, v = pr.sem_val
                        if water[s] < v:
                            engobj.wait_ge(sems[s], v)
                            water[s] = v
                    th = throttle.get(rec.idx)
                    if th is not None and water[th[0]] < th[1]:
                        engobj.wait_ge(sems[th[0]], th[1])
                        water[th[0]] = th[1]
                    insts = rec.emit(engobj)
                    if rec.sem_val is not None:
                        s, v = rec.sem_val
                        last = insts[-1] if isinstance(insts, (list, tuple)) else insts
                        last.then_inc(sems[s], 16 if rec.dma_q else 1)
                for q, owner in (("dY", "Y"), ("dA", "S"), ("dG", "G")):
                    if owner == ename:
                        for i in range(self.NPOOL):
                            if totals[(q, i)] > 0 and water[(q, i)] < totals[(q, i)]:
                                engobj.wait_ge(sems[(q, i)], totals[(q, i)])

            for ename, secname in (("Y", "sync"), ("S", "scalar"),
                                   ("V", "vector"), ("T", "tensor"),
                                   ("G", "gpsimd")):
                getattr(block, secname)(
                    lambda eo, _n=ename: emit_engine(_n, eo))
        return totals


def build_program(cfg: Cfg):
    nc = bass.Bass("TRN2", target_bir_lowering=False)
    NT, NO, EC, H, G, HT, DT, GT, NG = (cfg.NT, cfg.NO, cfg.EC, cfg.H,
                                        cfg.G, cfg.HT, cfg.DT, cfg.GT, cfg.NG)
    E, FH = cfg.E, cfg.FH
    NTOK = GT * 128

    d_x = nc.dram_tensor("x_full", [NT * 128, E], F32, kind="ExternalInput").ap()
    d_xo = nc.dram_tensor("x_own", [NO, 128, E], F32, kind="ExternalInput").ap()
    d_wq = nc.dram_tensor("wq_in", [EC, 128, H * 128], F32, kind="ExternalInput").ap()
    d_wkv = nc.dram_tensor("wkv_in", [EC, 128, 2 * G * 128], F32, kind="ExternalInput").ap()
    d_bq = nc.dram_tensor("bqT", [128, H], F32, kind="ExternalInput").ap()
    d_bk = nc.dram_tensor("bkT", [128, G], F32, kind="ExternalInput").ap()
    d_bv = nc.dram_tensor("bv_bcast", [128, G * 128], F32, kind="ExternalInput").ap()
    d_wg = nc.dram_tensor("w_g", [FH, E], F32, kind="ExternalInput").ap()
    d_wu = nc.dram_tensor("w_u", [FH, E], F32, kind="ExternalInput").ap()
    d_wd = nc.dram_tensor("w_d", [E, FH], F32, kind="ExternalInput").ap()
    d_mask = nc.dram_tensor("mask_tab", [NO, 128, 1024], BF16, kind="ExternalInput").ap()
    d_identb = nc.dram_tensor("identb", [128, 128], BF16, kind="ExternalInput").ap()
    d_identf = nc.dram_tensor("identf", [128, 128], F32, kind="ExternalInput").ap()
    d_onesb = nc.dram_tensor("onesb", [128, 8], BF16, kind="ExternalInput").ap()
    d_c127 = nc.dram_tensor("c127", [128, 8], F32, kind="ExternalInput").ap()

    d_vs = nc.dram_tensor("v_scr", [NT, 128, G * 128], BF16).ap()
    d_x1 = nc.dram_tensor("x1_scr", [NO, 128, E], F32).ap()
    d_sg = nc.dram_tensor("swg_scr", [HT, 128, E], BF16).ap()
    d_su = nc.dram_tensor("swu_scr", [HT, 128, E], BF16).ap()
    d_sd = nc.dram_tensor("swd_scr", [DT, 128, FH], BF16).ap()
    d_out = nc.dram_tensor("out", [NG, DT, 128, NTOK], F32, kind="ExternalOutput").ap()

    p = Prog(nc)
    SB = nc.alloc_sbuf_tensor
    PS = nc.alloc_psum_tensor

    identb = SB("sb_identb", [128, 128], BF16)
    onesb = SB("sb_onesb", [128, 8], BF16)
    c127 = SB("sb_c127", [128, 8], F32)
    bqs = SB("sb_bqs", [128, H], F32)
    bkT = SB("sb_bkT", [128, G], F32)
    xst = [SB(f"xst{i}", [128, E], F32) for i in range(2)]
    stats = SB("stats", [128, 32], F32)
    st2 = SB("st2", [128, 12], F32)

    KEY_B = [("PB", i) for i in range(8)]
    psum0 = (nc.psum_base, nc.psum_top)

    def psum_reset():
        nc.psum_base, nc.psum_top = psum0

    kvA = PS("kvA", [128, 1024], F32)
    kvB = PS("kvB", [128, 1024], F32)
    xtr1 = PS("xtr1", [128, 2048], BF16)
    ktr1 = PS("ktr1", [128, 1024], BF16)
    wtr = PS("wtr", [128, 1024], BF16)
    K_kvA, K_kvB = KEY_B[0:2], KEY_B[2:4]
    K_xtr1, K_ktr1, K_wtr = KEY_B[4:6], [KEY_B[6]], [KEY_B[7]]

    sb_savedA = (nc.sbuf_base, nc.sbuf_top)
    xln_tm = [SB(f"xln_tm{i}", [128, E], BF16) for i in range(2)]
    xln_fm = [SB(f"xln_fm{i}", [128, E], BF16) for i in range(2)]
    Kd = [SB(f"Kd{g}", [128, NT * 128], BF16) for g in range(G)]
    wst = [SB(f"wst{i}", [128, E], BF16) for i in range(2)]
    wtg = SB("wtg0", [128, E], BF16)
    sb_savedB = (nc.sbuf_base, nc.sbuf_top)
    Wkv_sb = SB("Wkv_sb", [128, EC, 2 * G * 128], BF16)
    ktm = SB("ktm", [128, G * 128], BF16)
    vtl = [SB(f"vtl{i}", [128, G * 128], BF16) for i in range(2)]
    bvb = SB("sb_bvb", [128, G * 128], F32)

    def ln_chain(src_key, src_ap, eps_col, out_ap, out_key):
        nchunk = E // 512
        p.op("V", lambda v: [v.bn_stats(stats[:, 6 * c:6 * c + 6],
                                        src_ap[:, 512 * c:512 * c + 512])
                             for c in range(nchunk)],
             reads=[src_key], writes=[("bn",)])
        p.op("V", lambda v: v.bn_aggr(st2[:, 0:2], stats[:, 0:6 * nchunk]),
             reads=[("bn",)], writes=[("mv",)])
        p.op("S", lambda s: s.activation(st2[:, 2:3], st2[:, 1:2], AF.Sqrt,
                                         bias=c127[:, eps_col:eps_col + 1]),
             reads=[("mv",), ("c127",)], writes=[("std",)])
        p.op("V", lambda v: v.reciprocal(st2[:, 3:4], st2[:, 2:3]),
             reads=[("std",)], writes=[("rstd",)])
        p.op("V", lambda v: v.tensor_scalar(out_ap, src_ap, st2[:, 0:1],
                                            st2[:, 3:4], ALU.subtract,
                                            ALU.mult),
             reads=[src_key, ("mv",), ("rstd",)], writes=[out_key])

    def ln_chain_s(src_key, src_ap, eps_col, out_ap, out_key):
        nchunk = E // 512
        p.op("V", lambda v: [v.bn_stats(stats[:, 6 * c:6 * c + 6],
                                        src_ap[:, 512 * c:512 * c + 512])
                             for c in range(nchunk)],
             reads=[src_key], writes=[("bn",)])
        p.op("V", lambda v: v.bn_aggr(st2[:, 0:2], stats[:, 0:6 * nchunk]),
             reads=[("bn",)], writes=[("mv",)])
        p.op("S", lambda s: s.activation(st2[:, 2:3], st2[:, 1:2], AF.Sqrt,
                                         bias=c127[:, eps_col:eps_col + 1]),
             reads=[("mv",), ("c127",)], writes=[("std",)])
        p.op("V", lambda v: v.reciprocal(st2[:, 3:4], st2[:, 2:3]),
             reads=[("std",)], writes=[("rstd",)])
        p.op("V", lambda v: v.tensor_scalar(st2[:, 8:9], st2[:, 0:1],
                                            st2[:, 3:4], -1.0,
                                            ALU.mult, ALU.mult),
             reads=[("mv",), ("rstd",)], writes=[("negmr",)])
        p.op("S", lambda s: s.activation(out_ap, src_ap, AF.Identity,
                                         bias=st2[:, 8:9], scale=st2[:, 3:4]),
             reads=[src_key, ("negmr",), ("rstd",)], writes=[out_key])

    def transpose16(src_ap, src_key, ps, ps_keys, dst_ap, dst_key):
        p.op("T", lambda t: [t.transpose(ps[:, 128 * c:128 * c + 128],
                                         src_ap[:, 128 * c:128 * c + 128],
                                         identb[:]) for c in range(EC)],
             reads=[src_key, ("identb",)], writes=ps_keys)
        p.op("V", lambda v: v.tensor_copy(dst_ap, ps[:, 0:128 * EC]),
             reads=ps_keys, writes=[dst_key])

    for sb_t, key, src in ((identb, "identb", d_identb),
                           (onesb, "onesb", d_onesb), (c127, "c127", d_c127),
                           (bqs, "bqs", d_bq), (bkT, "bkT", d_bk),
                           (bvb, "bvb", d_bv)):
        p.op("Y", lambda y, sb_t=sb_t, src=src: y.dma_start(out=sb_t[:], in_=src),
             writes=[(key,)], dma_q="dY")
    p.op("S", lambda s: s.mul(bqs[:], bqs[:], RSQ),
         reads=[("bqs",)], writes=[("bqs",)])
    for ech in range(EC):
        p.op("G", lambda g, c=ech: g.dma_start(out=Wkv_sb[:, c, :], in_=d_wkv[c]),
             writes=[("wkv", ech)], dma_q="dG")

    prep_srcs = []

    def prep_issue(k):
        src_ap, scr_ap, ncol, scr_key = prep_srcs[k]
        wb = k % 2
        p.op("G", lambda g, wb=wb, ncol=ncol, src_ap=src_ap: g.dma_start(
                out=wst[wb][:, 0:ncol], in_=src_ap),
             writes=[("wst", wb)], dma_q="dG")

    def prep_compute(k):
        src_ap, scr_ap, ncol, scr_key = prep_srcs[k]
        wb = k % 2
        nb = ncol // 128
        for h0 in range(0, nb, 8):
            hn = min(8, nb - h0)
            p.op("T", lambda t, h0=h0, hn=hn, wb=wb: [t.transpose(
                    wtr[:, 128 * c:128 * c + 128],
                    wst[wb][:, 128 * (h0 + c):128 * (h0 + c) + 128],
                    identb[:]) for c in range(hn)],
                 reads=[("wst", wb), ("identb",)], writes=K_wtr)
            p.op("S", lambda s, h0=h0, hn=hn: s.activation(
                    wtg[:, 128 * h0:128 * (h0 + hn)], wtr[:, 0:128 * hn],
                    AF.Sign),
                 reads=K_wtr, writes=[("wtg",)])
        p.op("G", lambda g, ncol=ncol, scr_ap=scr_ap: g.dma_start(
                out=scr_ap, in_=wtg[:, 0:ncol]),
             reads=[("wtg",)], writes=[scr_key], dma_q="dG")

    for ht in range(HT):
        prep_srcs.append((d_wg[128 * ht:128 * ht + 128, :], d_sg[ht], E,
                          ("dsg", ht)))
        prep_srcs.append((d_wu[128 * ht:128 * ht + 128, :], d_su[ht], E,
                          ("dsu", ht)))
    ncol_d = min(E, FH)
    for dt in range(DT):
        for hcg in range(max(1, FH // ncol_d)):
            prep_srcs.append((
                d_wd[128 * dt:128 * dt + 128, ncol_d * hcg:ncol_d * (hcg + 1)],
                d_sd[dt][:, ncol_d * hcg:ncol_d * (hcg + 1)], ncol_d,
                ("dsd", dt, hcg)))
    n_prep = len(prep_srcs)
    state = {"issued": 0, "done": 0}

    def drain_prep(k):
        while k > 0 and state["done"] < n_prep:
            while state["issued"] < min(n_prep, state["done"] + 2):
                prep_issue(state["issued"])
                state["issued"] += 1
            prep_compute(state["done"])
            state["done"] += 1
            k -= 1

    per = -(-n_prep // (NT + NO - 2)) if (NT + NO) > 2 else n_prep

    x_t = d_x.rearrange("(n p) e -> n p e", p=128)

    def ph1_head(t):
        b = t % 2
        p.op("Y", lambda y, t=t, b=b: y.dma_start(out=xst[b][:], in_=x_t[t]),
             writes=[("xst", b)], dma_q="dY")
        ln_chain(("xst", b), xst[b][:], 1, xln_tm[b][:], ("xln_tm", b))

    ph1_head(0)
    for t in range(NT):
        b = t % 2
        transpose16(xln_tm[b][:], ("xln_tm", b), xtr1, K_xtr1, xln_fm[b][:],
                    ("xln_fm", b))
        if t + 1 < NT:
            ph1_head(t + 1)
        kv = kvA if b == 0 else kvB
        K_kv = K_kvA if b == 0 else K_kvB

        def kv_mms(te, kv=kv, b=b):
            out = []
            for c in range(EC):
                for half in range(2):
                    out.append(te.matmul(
                        kv[:, 512 * half:512 * half + 512],
                        xln_fm[b][:, 128 * c:128 * c + 128],
                        Wkv_sb[:, c, 512 * half:512 * half + 512],
                        start=(c == 0), stop=(c == EC - 1)))
            return out
        p.op("T", kv_mms, reads=[("xln_fm", b)] + [("wkv", c) for c in range(EC)],
             writes=K_kv)
        p.op("S", lambda s, kv=kv: s.activation(ktm[:], kv[:, 0:512], AF.Copy),
             reads=K_kv, writes=[("ktm",)])
        p.op("T", lambda te: [te.transpose(ktr1[:, 128 * g:128 * g + 128],
                                           ktm[:, 128 * g:128 * g + 128],
                                           identb[:]) for g in range(G)],
             reads=[("ktm",), ("identb",)], writes=K_ktr1)
        p.op("S", lambda s, t=t: [s.activation(
                Kd[g][:, 128 * t:128 * t + 128],
                ktr1[:, 128 * g:128 * g + 128],
                AF.Identity, bias=bkT[:, g:g + 1]) for g in range(G)],
             reads=K_ktr1 + [("bkT",)], writes=[("Kd", t)])
        p.op("V", lambda v, kv=kv, b=b: v.tensor_tensor(vtl[b][:], kv[:, 512:1024],
                                                   bvb[:], ALU.add),
             reads=K_kv + [("bvb",)], writes=[("vtl", b)])
        p.op("S", lambda g, t=t, b=b: g.dma_start(out=d_vs[t], in_=vtl[b][:]),
             reads=[("vtl", b)], writes=[("dvs", t)], dma_q="dA")
        if t >= 1:
            drain_prep(per)

    nc.sbuf_base, nc.sbuf_top = sb_savedB
    Wq_sb = SB("Wq_sb", [128, EC, H * 128], BF16)
    etb = SB("etb", [128, NT * 512], BF16)
    v_sb = [SB(f"v_sb{i}", [128, NT * 128], BF16) for i in range(2)]
    Qg = SB("Qg", [128, G, 512], BF16)
    x1sb = SB("x1sb", [128, E], F32)
    msk = [SB(f"msk{i}", [128, 1024], BF16) for i in range(2)]
    rden = SB("rden", [128, 16], F32)

    psum_reset()
    s2 = [PS(f"s2_{i}", [128, 1024], F32) for i in range(2)]
    o_ps = [PS(f"o_ps{i}", [128, 512], F32) for i in range(2)]
    den_ps = PS("den_ps", [128, 512], F32)
    K_s2 = [KEY_B[0:2], KEY_B[2:4]]
    K_ops = [[KEY_B[4]], [KEY_B[5]]]
    K_den = [KEY_B[6]]
    psum_reset()
    xtr2 = PS("xtr2", [128, 2048], BF16)
    K_xtr2 = KEY_B[0:2]
    psum_reset()
    _dummy2 = PS("dummy2", [128, 1024], F32)
    q_ps = [PS(f"q_ps{g}", [128, 512], F32) for g in range(G)]
    K_qps = [[KEY_B[2 + g]] for g in range(G)]

    for ech in range(EC):
        extra = [("ktm",), ("vtl", 0), ("vtl", 1), ("bvb",)] if ech == 0 else []
        p.op("G", lambda g, c=ech: g.dma_start(out=Wq_sb[:, c, :], in_=d_wq[c]),
             writes=[("wq", ech), ("wkv", ech)] + extra, dma_q="dG")

    def ph2_head(j):
        b = j % 2
        p.op("Y", lambda y, j=j, b=b: y.dma_start(out=xst[b][:], in_=d_xo[j]),
             writes=[("xst", b)], dma_q="dY")
        p.op("Y", lambda y, j=j, b=b: y.dma_start(out=msk[b][:], in_=d_mask[j]),
             writes=[("msk", b)], dma_q="dY")
        ln_chain(("xst", b), xst[b][:], 1, xln_tm[b][:], ("xln_tm", b))

    ph2_head(0)
    for j in range(NO):
        b = j % 2
        nkt = 2 * j + 2
        transpose16(xln_tm[b][:], ("xln_tm", b), xtr2, K_xtr2, xln_fm[b][:],
                    ("xln_fm", b))

        def q_mms(te, b=b):
            out = []
            for c in range(EC):
                for h in range(H):
                    out.append(te.matmul(
                        q_ps[h // 4][:, 128 * (h % 4):128 * (h % 4) + 128],
                        Wq_sb[:, c, 128 * h:128 * h + 128],
                        xln_fm[b][:, 128 * c:128 * c + 128],
                        start=(c == 0 and h % 4 == 0),
                        stop=(c == EC - 1 and h % 4 == 3)))
            return out
        p.op("T", q_mms, reads=[("xln_fm", b)] + [("wq", c) for c in range(EC)],
             writes=sum(K_qps, []))
        p.op("S", lambda s: [s.activation(
                Qg[:, h // 4, 128 * (h % 4):128 * (h % 4) + 128],
                q_ps[h // 4][:, 128 * (h % 4):128 * (h % 4) + 128],
                AF.Identity, bias=bqs[:, h:h + 1], scale=RSQ) for h in range(H)],
             reads=sum(K_qps, []) + [("bqs",)], writes=[("Qg",)])
        if j + 1 < NO:
            ph2_head(j + 1)

        for g in range(G):
            vb = g % 2
            ob = g % 2
            p.op("Y", lambda y, g=g, vb=vb, nkt=nkt: y.dma_start(
                    out=v_sb[vb][:, 0:128 * nkt].rearrange(
                        "p (n d) -> p n d", d=128),
                    in_=d_vs[0:nkt, :, 128 * g:128 * g + 128].transpose([1, 0, 2])),
                 reads=[("dvs", kt) for kt in range(nkt)],
                 writes=[("vsb", vb)], dma_q="dY")

            def pv(te, kt, g=g, vb=vb, ob=ob, nkt=nkt):
                out = []
                st = kt == 0
                sp = kt == nkt - 1
                for h in range(4):
                    out.append(te.matmul(
                        o_ps[ob][:, 128 * h:128 * h + 128],
                        etb[:, 512 * kt + 128 * h:512 * kt + 128 * h + 128],
                        v_sb[vb][:, 128 * kt:128 * kt + 128],
                        start=(st and h == 0), stop=(sp and h == 3)))
                    out.append(te.matmul(
                        den_ps[:, 4 * g + h:4 * g + h + 1],
                        etb[:, 512 * kt + 128 * h:512 * kt + 128 * h + 128],
                        onesb[:, 0:1],
                        start=(st and h == 0), stop=(sp and h == 3)))
                return out

            def emit_pv(kt, pv=pv, vb=vb, ob=ob):
                p.op("T", lambda te, kt=kt, pv=pv: pv(te, kt),
                     reads=[("et", kt), ("vsb", vb), ("onesb",)],
                     writes=K_ops[ob] + K_den, waw_free=True)

            npair = nkt // 2
            for pp in range(npair):
                k0 = 2 * pp
                s2b = pp % 2
                last = pp == npair - 1
                for half in range(2):
                    p.op("T", lambda te, g=g, kt=k0 + half, s2b=s2b,
                         half=half, last=last: te.matmul(
                            s2[s2b][:, 512 * half:512 * half + 512],
                            Kd[g][:, 128 * kt:128 * kt + 128],
                            Qg[:, g, :], start=True, stop=not last),
                         reads=[("Kd", k0 + half), ("Qg",)],
                         writes=[K_s2[s2b][half]])
                    if last:
                        p.op("T", lambda te, s2b=s2b, half=half, b=b: te.matmul(
                                s2[s2b][:, 512 * half:512 * half + 512],
                                identb[:],
                                msk[b][:, 512 * half:512 * half + 512],
                                start=False, stop=True),
                             reads=[("msk", b), ("identb",)],
                             writes=[K_s2[s2b][half]], waw_free=True)
                p.op("S", lambda s, k0=k0, s2b=s2b: s.activation(
                        etb[:, 512 * k0:512 * k0 + 1024], s2[s2b][:],
                        AF.Exp, bias=c127[:, 3:4]),
                     reads=K_s2[s2b] + [("c127",)],
                     writes=[("et", k0), ("et", k0 + 1)])
                if pp >= 1:
                    emit_pv(k0 - 2)
                    emit_pv(k0 - 1)
            emit_pv(nkt - 2)
            emit_pv(nkt - 1)

            p.op("V", lambda v, g=g: v.reciprocal(rden[:, 4 * g:4 * g + 4],
                                                  den_ps[:, 4 * g:4 * g + 4]),
                 reads=K_den, writes=[("rden", g)])
            p.op("V", lambda v, g=g, ob=ob, b=b: [v.scalar_tensor_tensor(
                    x1sb[:, 512 * g + 128 * h:512 * g + 128 * h + 128],
                    o_ps[ob][:, 128 * h:128 * h + 128],
                    rden[:, 4 * g + h:4 * g + h + 1],
                    xst[b][:, 512 * g + 128 * h:512 * g + 128 * h + 128],
                    ALU.mult, ALU.add) for h in range(4)],
                 reads=K_ops[ob] + [("rden", g), ("xst", b)],
                 writes=[("x1sb", g)])
        p.op("S", lambda g, j=j: g.dma_start(out=d_x1[j], in_=x1sb[:]),
             reads=[("x1sb", gg) for gg in range(G)],
             writes=[("dx1", j)], dma_q="dA")
        drain_prep(per)
    drain_prep(n_prep)

    p.barrier()
    nc.sbuf_base, nc.sbuf_top = sb_savedA
    identf = SB("sb_identf", [128, 128], F32)
    x1f = SB("x1f", [128, DT * NTOK], F32)
    xqf = SB("xqf", [128, EC * NTOK], BF16)
    t1 = SB("t1", [128, E], F32)
    i32 = SB("i32", [128, 512], I32)
    xq_tm = SB("xq_tm", [128, E], BF16)
    pgrp = SB("pgrp", [128, HT * NTOK], BF16)
    swg_t = [SB(f"swg{i}", [128, E], BF16) for i in range(2)]
    swu_t = [SB(f"swu{i}", [128, E], BF16) for i in range(2)]
    swd_t = [SB(f"swd{i}", [128, FH], BF16) for i in range(2)]
    g_sb = SB("g_sb", [128, 512], BF16)
    out_sb = [SB(f"out_sb{i}", [128, NTOK], F32) for i in range(2)]

    psum_reset()
    x1tr = PS("x1tr", [128, 512], F32)
    xqtr = PS("xqtr", [128, 2048], BF16)
    g_ps = [PS(f"g_ps{i}", [128, 512], F32) for i in range(2)]
    u_ps = [PS(f"u_ps{i}", [128, 512], F32) for i in range(2)]
    d_ps0 = PS("d_ps0", [128, 512], F32)
    K_x1tr = [KEY_B[0]]
    K_xqtr = KEY_B[1:3]
    K_gps = [[KEY_B[3]], [KEY_B[4]]]
    K_ups = [[KEY_B[5]], [KEY_B[6]]]
    d_ps = [d_ps0, x1tr]
    K_dps = [[KEY_B[7]], [KEY_B[0]]]

    p.op("Y", lambda y: y.dma_start(out=identf[:], in_=d_identf),
         writes=[("identf",)], dma_q="dY")
    x1f3 = x1f[:].rearrange("p (d t) -> p d t", t=NTOK)
    xqf3 = xqf[:].rearrange("p (c t) -> p c t", t=NTOK)

    for grp in range(NG):
        for jj in range(GT):
            j4 = GT * grp + jj
            b = jj % 2
            p.op("Y", lambda y, j4=j4, b=b: y.dma_start(out=xst[b][:],
                                                        in_=d_x1[j4]),
                 reads=[("dx1", j4)], writes=[("xst", b)], dma_q="dY")
            for r in range(4):
                p.op("T", lambda t, b=b, r=r: [t.transpose(
                        x1tr[:, 128 * c:128 * c + 128],
                        xst[b][:, 512 * r + 128 * c:512 * r + 128 * c + 128],
                        identf[:]) for c in range(4)],
                     reads=[("xst", b), ("identf",)], writes=K_x1tr)
                p.op("V", lambda v, r=r, jj=jj: v.tensor_copy(
                        x1f3[:, 4 * r:4 * r + 4, 128 * jj:128 * jj + 128],
                        x1tr[:, 0:512].rearrange("p (a q) -> p a q", q=128)),
                     reads=K_x1tr, writes=[("x1f", jj, r)])
            ln_chain_s(("xst", b), xst[b][:], 1, t1[:], ("t1",))
            ln_chain_s(("t1",), t1[:], 2, t1[:], ("t1",))
            p.op("V", lambda v: v.reduce_max(st2[:, 4:5], t1[:], axis=AX,
                                             apply_absolute_value=True),
                 reads=[("t1",)], writes=[("am",)])
            p.op("V", lambda v: v.reciprocal(st2[:, 7:8], st2[:, 4:5]),
                 reads=[("am",)], writes=[("ram",)])
            p.op("V", lambda v: v.tensor_scalar(st2[:, 5:6], st2[:, 7:8],
                                                127.0, None, ALU.mult),
                 reads=[("ram",)], writes=[("s",)])
            p.op("V", lambda v: v.tensor_scalar(st2[:, 6:7], st2[:, 4:5],
                                                float(1.0 / 127.0), None,
                                                ALU.mult),
                 reads=[("am",)], writes=[("rs",)])
            for c in range(4):
                sl = slice(512 * c, 512 * c + 512)
                p.op("V", lambda v, sl=sl: v.tensor_scalar(
                        t1[:, sl], t1[:, sl], st2[:, 5:6], MAGIC,
                        ALU.mult, ALU.add),
                     reads=[("t1",), ("s",)], writes=[("t1",)])
                p.op("V", lambda v, sl=sl: v.tensor_scalar(
                        i32[:], t1[:, sl], MAGIC, None, ALU.subtract),
                     reads=[("t1",)], writes=[("i32",)])
                p.op("V", lambda v, sl=sl: v.tensor_scalar(
                        xq_tm[:, sl], i32[:], st2[:, 6:7], None, ALU.mult),
                     reads=[("i32",), ("rs",)], writes=[("xq_tm",)])
            p.op("T", lambda t: [t.transpose(xqtr[:, 128 * c:128 * c + 128],
                                             xq_tm[:, 128 * c:128 * c + 128],
                                             identb[:]) for c in range(EC)],
                 reads=[("xq_tm",), ("identb",)], writes=K_xqtr)
            p.op("V", lambda v, jj=jj: v.tensor_copy(
                    xqf3[:, :, 128 * jj:128 * jj + 128],
                    xqtr[:, 0:128 * EC].rearrange("p (c q) -> p c q", q=128)),
                 reads=K_xqtr, writes=[("xqf", jj)])

        xq_keys = [("xqf", jj) for jj in range(GT)]
        x1_keys = [("x1f", jj, r) for jj in range(GT) for r in range(4)]
        for ht in range(HT):
            hb = ht % 2
            p.op("S", lambda s, ht=ht, hb=hb: s.dma_start(
                    out=swg_t[hb][:], in_=d_sg[ht]),
                 reads=[("dsg", ht)], writes=[("swg", hb)], dma_q="dA")
            p.op("S", lambda s, ht=ht, hb=hb: s.dma_start(
                    out=swu_t[hb][:], in_=d_su[ht]),
                 reads=[("dsu", ht)], writes=[("swu", hb)], dma_q="dA")

            def gu(te, wt, pp):
                return [te.matmul(pp[:, 0:NTOK], wt[:, 128 * c:128 * c + 128],
                                  xqf[:, NTOK * c:NTOK * c + NTOK],
                                  start=(c == 0), stop=(c == EC - 1))
                        for c in range(EC)]
            p.op("T", lambda te, hb=hb: gu(te, swg_t[hb], g_ps[hb]),
                 reads=[("swg", hb)] + xq_keys, writes=K_gps[hb])
            p.op("T", lambda te, hb=hb: gu(te, swu_t[hb], u_ps[hb]),
                 reads=[("swu", hb)] + xq_keys, writes=K_ups[hb])
            p.op("S", lambda s, hb=hb: s.activation(
                    g_sb[:, 0:NTOK], g_ps[hb][:, 0:NTOK], AF.Silu),
                 reads=K_gps[hb], writes=[("g_sb",)])
            p.op("V", lambda v, ht=ht, hb=hb: v.tensor_tensor(
                    pgrp[:, NTOK * ht:NTOK * ht + NTOK],
                    u_ps[hb][:, 0:NTOK], g_sb[:, 0:NTOK], ALU.mult),
                 reads=K_ups[hb] + [("g_sb",)], writes=[("pg", ht)])

        pg_keys = [("pg", ht) for ht in range(HT)]
        for dt in range(DT):
            db = dt % 2
            p.op("S", lambda s, dt=dt, db=db: s.dma_start(
                    out=swd_t[db][:], in_=d_sd[dt]),
                 reads=[("dsd", dt, h) for h in range(max(1, FH // ncol_d))],
                 writes=[("swd", db)], dma_q="dA")
            p.op("T", lambda te, dt=dt, db=db: [te.matmul(
                    d_ps[db][:, 0:NTOK], swd_t[db][:, 128 * hc:128 * hc + 128],
                    pgrp[:, NTOK * hc:NTOK * hc + NTOK],
                    start=(hc == 0), stop=(hc == HT - 1)) for hc in range(HT)],
                 reads=[("swd", db)] + pg_keys, writes=K_dps[db])
            p.op("V", lambda v, dt=dt, db=db: v.tensor_tensor(
                    out_sb[db][:], d_ps[db][:, 0:NTOK],
                    x1f[:, NTOK * dt:NTOK * dt + NTOK], ALU.add),
                 reads=K_dps[db] + x1_keys, writes=[("osb", db)])
            p.op("Y", lambda g, grp=grp, dt=dt, db=db: g.dma_start(
                    out=d_out[grp, dt], in_=out_sb[db][:]),
                 reads=[("osb", db)], dma_q="dY")

    p.finalize()
    return nc


def _prep_inputs(cfg, x_b, par, Wq, bq, Wk, bk, Wv, bv, Wg, Wu, Wd, masks):
    NT, EC, H, G, E = cfg.NT, cfg.EC, cfg.H, cfg.G, cfg.E
    m = {}
    m["x_full"] = np.ascontiguousarray(x_b)
    m["x_own"] = np.ascontiguousarray(x_b.reshape(NT, 128, E)[par::2])
    m["wq_in"] = np.ascontiguousarray(
        Wq.reshape(H, EC, 128, 128).transpose(1, 2, 0, 3)
          .reshape(EC, 128, H * 128))
    wkv = np.concatenate([Wk, Wv], axis=0)
    m["wkv_in"] = np.ascontiguousarray(
        wkv.reshape(2 * G, EC, 128, 128).transpose(1, 2, 0, 3)
           .reshape(EC, 128, 2 * G * 128))
    m["bqT"] = np.ascontiguousarray(bq.T)
    m["bkT"] = np.ascontiguousarray(bk.T)
    m["bv_bcast"] = np.ascontiguousarray(
        np.broadcast_to(bv.reshape(1, G * 128), (128, G * 128)))
    m["w_g"] = np.ascontiguousarray(Wg)
    m["w_u"] = np.ascontiguousarray(Wu)
    m["w_d"] = np.ascontiguousarray(Wd)
    m["mask_tab"] = masks
    m["identb"] = np.eye(128, dtype=BF16NP)
    m["identf"] = np.eye(128, dtype=np.float32)
    m["onesb"] = np.ones((128, 8), dtype=BF16NP)
    cc = np.zeros((128, 8), dtype=np.float32)
    cc[:, 0] = 127.0
    cc[:, 1] = EPS1
    cc[:, 2] = EPS2
    cc[:, 3] = -EXPC
    m["c127"] = cc
    return m


def _masks(cfg, par):
    NO = cfg.NO
    NEG = np.float32(-1e30)
    tab = np.zeros((NO, 128, 2, 4, 128), dtype=BF16NP)
    kp = np.arange(128)[:, None]
    qf = np.arange(128)[None, :]
    tri = np.where(kp <= qf, np.float32(0.0), NEG).astype(BF16NP)
    for j in range(NO):
        if par == 0:
            tab[j, :, 0] = tri[:, None, :]
            tab[j, :, 1] = NEG.astype(BF16NP)
        else:
            tab[j, :, 0] = 0.0
            tab[j, :, 1] = tri[:, None, :]
    return np.ascontiguousarray(tab.reshape(NO, 128, 1024))


def kernel(L0_attn_x, L0_attn_Wq, L0_attn_bq, L0_attn_Wk, L0_attn_bk,
           L0_attn_Wv, L0_attn_bv, L0_ffn_W_g, L0_ffn_W_u, L0_ffn_W_d):
    cfg = Cfg()
    x = np.asarray(L0_attn_x, dtype=np.float32)
    B, S, E = x.shape
    nc = build_program(cfg)
    args = [np.asarray(a, np.float32) for a in
            (L0_attn_Wq, L0_attn_bq, L0_attn_Wk, L0_attn_bk,
             L0_attn_Wv, L0_attn_bv, L0_ffn_W_g, L0_ffn_W_u, L0_ffn_W_d)]
    in_maps = []
    for c in range(8):
        b, par = c // 2, c % 2
        in_maps.append(_prep_inputs(cfg, x[b], par, *args, _masks(cfg, par)))
    res = run_bass_kernel_spmd(nc, in_maps, core_ids=list(range(8)))
    out = np.empty((B, S, E), dtype=np.float32)
    NO, DT, GT, NG = cfg.NO, cfg.DT, cfg.GT, cfg.NG
    for c in range(8):
        b, par = c // 2, c % 2
        o = np.asarray(res.results[c]["out"])
        o = o.reshape(NG, DT, 128, GT, 128).transpose(0, 3, 4, 1, 2)
        o = o.reshape(NO, 128, DT * 128)
        out[b].reshape(cfg.NT, 128, E)[par::2] = o
    return out
